# version 06
# baseline (speedup 1.0000x reference)
import os
import numpy as np
from contextlib import ExitStack

import concourse.bass as bass
import concourse.mybir as mybir
from concourse.bass_utils import run_bass_kernel_spmd

F32 = mybir.dt.float32
BF16 = mybir.dt.bfloat16
Alu = mybir.AluOpType
Act = mybir.ActivationFunctionType
AX = mybir.AxisListType.X

MAGIC = float(np.float32(1.5 * 2 ** 23))
T, D, DH, H = 4096, 1024, 4096, 16
SUB, NSUB = 512, 8
WIN, NWIN = 256, 16
EPS_ATTN = 1e-6
EPS_LN = 1e-5


class Sch:
    ENGINES = ("sync", "tensor", "vector", "scalar", "gpsimd")

    def __init__(self, nc):
        self.nc = nc
        self.q = {e: [] for e in self.ENGINES}
        self.sem = {}
        self.cnt = {}
        self.waited = {e: {} for e in self.ENGINES}

    def add_sem(self, name, handle):
        self.sem[name] = handle
        self.cnt[name] = 0

    def op(self, eng, fn, waits=(), inc=None, amt=1):
        real = []
        for wv in waits:
            if wv is None:
                continue
            s, v = wv
            w = self.waited[eng]
            if v > w.get(s, 0):
                w[s] = v
                real.append((s, v))
        tok = None
        if inc is not None:
            self.cnt[inc] += amt
            tok = (inc, self.cnt[inc])
        self.q[eng].append((real, fn, tok, amt))
        return tok

    def dma(self, eng, out, in_, waits=(), **kw):
        sem = {"sync": "dsy", "scalar": "dsc", "gpsimd": "dgp"}[eng]
        waits = list(waits)
        waits.append((sem, self.cnt[sem]))
        return self.op(eng, lambda e: e.dma_start(out=out, in_=in_, **kw),
                       waits, inc=sem, amt=16)

    def replay(self, block):
        for eng in self.ENGINES:
            ops = list(self.q[eng])
            if not ops:
                continue

            def body(e, ops_local=ops):
                for waits, fn, tok, amt in ops_local:
                    for s, v in waits:
                        e.wait_ge(self.sem[s], v)
                    inst = fn(e)
                    if tok is not None:
                        inst.then_inc(self.sem[tok[0]], amt)
            getattr(block, eng)(body)
            self.q[eng] = []

    def barrier(self):
        toks = []
        for eng, sem in [("sync", "sy"), ("tensor", "pe"), ("vector", "ve"),
                         ("scalar", "se"), ("gpsimd", "po")]:
            toks.append(self.op(eng, lambda e: e.drain(), inc=sem))
        for s in ("dsy", "dsc", "dgp"):
            toks.append((s, self.cnt[s]))
        for eng in self.ENGINES:
            self.op(eng, lambda e: e.nop(), waits=toks)
        return toks


def build_nc(debug_tap=None, max_sub=NSUB, do_b=True, max_win=NWIN, stage_a=99, max_seg=8, seg_stage=99, seg2_stage=99):
    nc = bass.Bass("TRN2", target_bir_lowering=False)

    x_h = nc.dram_tensor("x", [T, D], F32, kind="ExternalInput").ap()
    wqT_h = nc.dram_tensor("wqT", [D, D], F32, kind="ExternalInput").ap()
    wkT_h = nc.dram_tensor("wkT", [D, D], F32, kind="ExternalInput").ap()
    wvT_h = nc.dram_tensor("wvT", [D, D], F32, kind="ExternalInput").ap()
    woT_h = nc.dram_tensor("woT", [D, D], F32, kind="ExternalInput").ap()
    beta_h = nc.dram_tensor("beta", [1, H], F32, kind="ExternalInput").ap()
    w1T_h = nc.dram_tensor("w1T", [D, DH], F32, kind="ExternalInput").ap()
    cwr_h = nc.dram_tensor("cwr", [128, 96], F32, kind="ExternalInput").ap()
    cbr_h = nc.dram_tensor("cbr", [128, 32], F32, kind="ExternalInput").ap()
    w2T_h = nc.dram_tensor("w2T", [DH, D], F32, kind="ExternalInput").ap()
    out_h = nc.dram_tensor("out", [T, D], F32, kind="ExternalOutput").ap()

    xq1s_h = nc.dram_tensor("xq1s", [D, T], BF16, kind="Internal").ap()
    w1ts_h = nc.dram_tensor("w1ts", [D, DH], BF16, kind="Internal").ap()
    w2ts_h = nc.dram_tensor("w2ts", [DH, D], BF16, kind="Internal").ap()
    gs_h = nc.dram_tensor("gs", [1, T], F32, kind="Internal").ap()

    dbg_h = None
    if debug_tap is not None:
        dbg_h = nc.dram_tensor("dbg", list(debug_tap[1]), debug_tap[2],
                               kind="ExternalOutput").ap()

    es = ExitStack()
    sb = lambda n, sh, dt: es.enter_context(nc.sbuf_tensor(n, sh, dt))

    sch = Sch(nc)
    for s in ("dsy", "dsc", "dgp", "pe", "ve", "se", "po", "sy"):
        sch.add_sem(s, es.enter_context(nc.semaphore(s)))

    pA = es.enter_context(nc.psum_tensor("pA", [128, 2048], F32))
    pB = es.enter_context(nc.psum_tensor("pB", [128, 2048], F32))
    idb = sb("idb", [128, 128], BF16)
    idf = sb("idf", [128, 128], F32)
    m01 = sb("m01", [128, 64], BF16)
    m01f = sb("m01f", [128, 64], F32)
    bt = sb("bt", [128, 16], F32)
    btm = sb("btm", [128, 16], F32)
    prs = sb("prs", [128, 8], F32)
    onesf = sb("onesf", [1, 128], F32)
    r1row = sb("r1row", [1, 128], F32)

    taps = {}

    def tap(name, ap, waits):
        if debug_tap is not None and debug_tap[0] == name:
            taps[name] = sch.dma("sync", dbg_h, ap, waits=waits)

    def pa(b, c0=0, c1=512):
        return pA[:, 512 * b + c0:512 * b + c1]

    def pb(b, c0=0, c1=512):
        return pB[:, 512 * b + c0:512 * b + c1]

    def pac(b, c0=0, c1=512):
        return pA[0:64, 512 * b + c0:512 * b + c1]

    def pbc(b, c0=0, c1=512):
        return pB[0:64, 512 * b + c0:512 * b + c1]

    def pab(b):
        return pA[:, 512 * b:512 * (b + 1)].bitcast(BF16)

    def pbb(b):
        return pB[:, 512 * b:512 * (b + 1)].bitcast(BF16)

    TP = lambda hp: (64 * hp, 0) if hp else None

    with es:
        with ExitStack() as eA:
            sbA = lambda n, sh, dt: eA.enter_context(nc.sbuf_tensor(n, sh, dt))
            wq_t = sbA("wq_t", [128, 8, D], BF16)
            wk_t = sbA("wk_t", [128, 8, D], BF16)
            wv_t = sbA("wv_t", [128, 8, D], BF16)
            wo_t = sbA("wo_t", [128, 8, D], BF16)
            eP = ExitStack()
            sbP = lambda n, sh, dt: eP.enter_context(nc.sbuf_tensor(n, sh, dt))
            bsm = sbP("bsm", [1, 16], F32)
            wsl = [sbP(f"wsl{i}", [128, DH], F32) for i in range(2)]
            wsb = [sbP(f"wsb{i}", [128, DH], BF16) for i in range(2)]
            slr = sbP("slr", [128, 40], F32)
            tot = sbP("tot", [1, 8], F32)

            sch.op("gpsimd", lambda e: e.memset(onesf[:], 1.0))
            tm1 = sch.op("gpsimd", lambda e: e.memset(idb[:], 0.0), inc="po")
            tm2 = sch.op("gpsimd", lambda e: e.memset(idf[:], 0.0), inc="po")
            tm3 = sch.op("gpsimd", lambda e: e.memset(m01f[:], 1.0), inc="po")
            sch.op("gpsimd", lambda e: e.affine_select(
                out=idb[:], in_=idb[:], compare_op=Alu.not_equal, fill=1.0,
                base=0, pattern=[[-1, 128]], channel_multiplier=1),
                waits=[tm1], inc="po")
            sch.op("gpsimd", lambda e: e.affine_select(
                out=idf[:], in_=idf[:], compare_op=Alu.not_equal, fill=1.0,
                base=0, pattern=[[-1, 128]], channel_multiplier=1),
                waits=[tm2], inc="po")
            t_m01f = tm3
            for half in range(2):
                t_m01f = sch.op("gpsimd", lambda e, h0=64 * half: e.affine_select(
                    out=m01f[h0:h0 + 64, :], in_=m01f[h0:h0 + 64, :],
                    compare_op=Alu.is_ge, fill=0.0, base=0,
                    pattern=[[-1, 64]], channel_multiplier=1),
                    waits=[t_m01f], inc="po")
            sch.op("vector", lambda e: e.tensor_copy(out=m01[:], in_=m01f[:]),
                   waits=[t_m01f], inc="ve")
            t_ones = sch.op("vector", lambda e: e.memset(slr[:, 39:40], 1.0),
                            inc="ve")
            t_wsb0 = sch.op("vector", lambda e: e.memset(wsb[0][:], 0.0), inc="ve")
            t_wsb1 = sch.op("vector", lambda e: e.memset(wsb[1][:], 0.0), inc="ve")
            t_beta = sch.dma("gpsimd", bsm[:], beta_h)
            t_sig = sch.op("scalar", lambda e: e.activation(
                out=bsm[:], in_=bsm[:], func=Act.Sigmoid), waits=[t_beta],
                inc="se")
            t_btmm = sch.op("tensor", lambda e: e.matmul(
                pA[:, 512:528], onesf[:], bsm[:], start=True, stop=True),
                waits=[t_sig, t_ones], inc="pe")
            t_bt = sch.op("scalar", lambda e: e.copy(
                out=bt[:], in_=pA[:, 512:528]), waits=[t_btmm], inc="se")
            t_btm = sch.op("vector", lambda e: e.tensor_scalar(
                out=btm[:], in0=bt[:], scalar1=-1.0, scalar2=1.0,
                op0=Alu.mult, op1=Alu.add), waits=[t_bt], inc="ve")
            t_prs_prev = t_bt
            for w_h, n_sl, wts_h, sc in ((w1T_h, 8, w1ts_h, 0), (w2T_h, 32, w2ts_h, 2)):
                ncols = w_h.shape[1]
                w3 = w_h.rearrange("(k p) c -> p k c", p=128)
                wts3 = wts_h.rearrange("(k p) c -> p k c", p=128)
                t_vsnap = ("ve", sch.cnt["ve"])
                t_dsnap = ("dsy", sch.cnt["dsy"])
                reds = []
                for s in range(n_sl):
                    buf = wsl[s % 2]
                    wv = [reds[s - 2]] if s >= 2 else [t_vsnap]
                    tl = sch.dma("scalar", buf[:, 0:ncols], w3[:, s, :], waits=wv)
                    reds.append(sch.op("vector", lambda e, b=buf, s_=s, nc2=ncols: e.tensor_reduce(
                        out=slr[:, s_:s_ + 1], in_=b[:, 0:nc2], axis=AX,
                        op=Alu.add, apply_absolute_value=True),
                        waits=[tl], inc="ve"))
                tsum = sch.op("vector", lambda e, n=n_sl: e.tensor_reduce(
                    out=slr[:, 38:39], in_=slr[:, 0:n], axis=AX, op=Alu.add),
                    waits=[reds[-1]], inc="ve")
                tmm = sch.op("tensor", lambda e: e.matmul(
                    pA[0:1, 0:1], slr[:, 38:39], slr[:, 39:40],
                    start=True, stop=True), waits=[tsum, t_ones], inc="pe")
                tcp = sch.op("scalar", lambda e: e.copy(
                    out=tot[0:1, 0:1], in_=pA[0:1, 0:1]), waits=[tmm], inc="se")
                t2 = sch.op("vector", lambda e, nc_=ncols, n=n_sl: e.tensor_scalar(
                    out=tot[0:1, 1:2], in0=tot[0:1, 0:1],
                    scalar1=1.0 / (nc_ * n * 128), scalar2=1e-5,
                    op0=Alu.mult, op1=Alu.max), waits=[tcp], inc="ve")
                t3 = sch.op("vector", lambda e: e.reciprocal(
                    out=tot[0:1, 2:3], in_=tot[0:1, 1:2]), waits=[t2], inc="ve")
                t4mm = sch.op("tensor", lambda e: e.matmul(
                    pA[:, 528:530], onesf[:], tot[0:1, 1:3], start=True,
                    stop=True, skip_group_check=True),
                    waits=[t3, t_prs_prev], inc="pe")
                t4 = sch.op("scalar", lambda e, c=sc: e.copy(
                    out=prs[:, c:c + 2], in_=pA[:, 528:530]), waits=[t4mm],
                    inc="se")
                t_prs_prev = t4
                st_toks = [None, None]
                for s in range(n_sl):
                    buf = wsl[s % 2]
                    obf = wsb[s % 2]
                    wv = [t4, reds[-1]]
                    if s >= 2:
                        wv.append(st_toks[s % 2])
                    tl = sch.dma("scalar", buf[:, 0:ncols], w3[:, s, :], waits=wv)
                    tr1 = sch.op("vector", lambda e, b=buf, c=sc, nc2=ncols: e.tensor_scalar(
                        out=b[:, 0:nc2], in0=b[:, 0:nc2],
                        scalar1=prs[:, c + 1:c + 2], scalar2=MAGIC,
                        op0=Alu.mult, op1=Alu.add), waits=[tl], inc="ve")
                    tr2 = sch.op("vector", lambda e, b=buf, nc2=ncols: e.tensor_scalar(
                        out=b[:, 0:nc2], in0=b[:, 0:nc2], scalar1=MAGIC,
                        scalar2=1.0, op0=Alu.subtract, op1=Alu.min),
                        waits=[tr1], inc="ve")
                    tr3 = sch.op("vector", lambda e, b=buf, o=obf, nc2=ncols: e.tensor_scalar(
                        out=o[:, 0:nc2], in0=b[:, 0:nc2], scalar1=-1.0,
                        scalar2=None, op0=Alu.max),
                        waits=[tr2, t_wsb0, t_wsb1, t_dsnap, st_toks[s % 2]],
                        inc="ve")
                    st_toks[s % 2] = sch.dma("sync", wts3[:, s, :],
                                             obf[:, 0:ncols], waits=[tr3])

            for w_h, wsb_ in ((wqT_h, wq_t), (wkT_h, wk_t), (wvT_h, wv_t),
                              (woT_h, wo_t)):
                sch.dma("gpsimd", wsb_[:], w_h.rearrange("(k p) c -> p k c", p=128))

            bar0 = sch.barrier()
            sch.op("vector", lambda e: e.tensor_scalar(
                out=r1row[:], in0=onesf[:], scalar1=prs[0:1, 0:1], scalar2=None,
                op0=Alu.mult), waits=[bar0[0]], inc="ve")
            with eP:
                blkP = eP.enter_context(nc.Block())
                sch.replay(blkP)

            xa = sbA("xa0", [128, 4, D], F32)
            xn = sbA("xn", [128, 4, D], F32)
            st = sbA("st", [128, 40], F32)
            xq1t = sbA("xq1t", [128, 4, D], BF16)
            xnf = sbA("xnf", [128, 8, SUB], BF16)
            q1f = sbA("q1f", [128, 8, SUB], BF16)
            qf = sbA("qf", [128, 8, SUB], BF16)
            kf = sbA("kf", [128, 8, SUB], BF16)
            vt = xn[:].bitcast(BF16)[:, :, 0:D]
            sqf = sbA("sqf", [128, 8, SUB], BF16)
            skf = sbA("skf", [128, 8, SUB], BF16)
            kz = sbA("kz", [128, 16, 64], BF16)
            skt = xnf[:].rearrange("p k t -> p (k t)").rearrange(
                "p (a b) -> p a b", a=4)
            tmp1 = q1f[:]
            skt_sw = q1f[:].rearrange("p k t -> p (k t)").rearrange(
                "p (a b) -> p a b", a=4)
            vt_sw = xq1t[:]
            asf = xq1t[:].rearrange("p a b -> p (a b)").rearrange(
                "p (k t) -> p k t", k=8)
            psb = sbA("psb", [128, 1024], BF16)
            ptr = sbA("ptr", [128, 1024], BF16)
            vmvp = sbA("vmvp", [128, 1024], BF16)
            m32 = sbA("m32", [128, 16, 64], F32)
            msbf = sbA("msbf", [128, 16, 64], BF16)
            ast = sbA("ast", [128, 8, D], BF16)
            junk = ast[:].rearrange("p s d -> p (s d)")[:, 0:2048].bitcast(F32)
            outsb = sbA("outsb", [128, 2, D], F32)
            zsb = sbA("zsb", [128, 8], F32)
            zc = sbA("zc", [128, 64], F32)
            zmm = sbA("zmm", [128, 8, 16], BF16)
            denb = sbA("denb", [128, 8, 32], F32)
            nqb = sbA("nqb", [128, 8, 16], F32)
            mxs = sbA("mxs", [128, 16], F32)
            sms = sbA("sms", [128, 16], F32)
            nsm = sbA("nsm", [128, 16], BF16)
            tasb = sbA("tasb", [128, 512], F32)
            tas = tasb[0:64, :]

            t_z0 = sch.op("gpsimd", lambda e: e.memset(zsb[:], 0.0),
                          waits=[bar0[0]], inc="po")
            t_kz0 = sch.op("gpsimd", lambda e: e.memset(kz[:], 0.0),
                           waits=[bar0[0]], inc="po")
            t_zmm0 = sch.op("gpsimd", lambda e: e.memset(zmm[:], 0.0),
                            waits=[bar0[0]], inc="po")
            t_ptr0 = sch.op("gpsimd", lambda e: e.memset(ptr[64:128, :], 0.0),
                            waits=[bar0[0]], inc="po")
            t_vm0 = sch.op("gpsimd", lambda e: e.memset(vmvp[64:128, :], 0.0),
                           waits=[bar0[0]], inc="po")
            t_madd = sch.op("vector", lambda e: e.memset(m32[:], 0.0),
                            waits=[bar0[0]], inc="ve")
            t_mcast = sch.op("scalar", lambda e: e.copy(
                out=msbf[:], in_=m32[:]), waits=[t_madd], inc="se")

            x3 = x_h.rearrange("(s t p) d -> s p t d", p=128, t=4)
            out3 = out_h.rearrange("(s t p) d -> s p t d", p=128, t=4)
            xq1s3 = xq1s_h.rearrange("(k p) t -> p k t", p=128)
            gs2 = gs_h.rearrange("o (s t p) -> o s p t", p=128, t=4)

            tok_xfree = [None] * NSUB
            t_pa_free = bar0[0]
            t_pb_free = bar0[0]
            t_out_toks = []
            t_outdma_prev = bar0[0]
            t_segloop_prev = bar0[0]
            t_den_prev = bar0[0]

            for u in range(max_sub):
                xb = xa
                wv = [tok_xfree[u - 1]] if u >= 1 else [bar0[0]]
                tx = sch.dma("sync", xa[:], x3[u], waits=wv)

                t_sum = sch.op("vector", lambda e, b=xb: e.tensor_reduce(
                    out=st[:, 0:4], in_=b[:], axis=AX, op=Alu.add),
                    waits=[tx], inc="ve")
                t_sq = None
                for tt in range(4):
                    t_sq = sch.op("scalar", lambda e, b=xb, t_=tt: e.activation(
                        out=junk[:], in_=b[:, t_, :], func=Act.Square,
                        accum_out=st[:, 4 + t_:5 + t_]),
                        waits=[tx] + ([t_sq] if t_sq else []), inc="se")
                t_mean = sch.op("vector", lambda e: e.tensor_scalar(
                    out=st[:, 8:12], in0=st[:, 0:4], scalar1=1.0 / D,
                    scalar2=None, op0=Alu.mult), waits=[t_sum], inc="ve")
                t_v1 = sch.op("vector", lambda e: e.tensor_mul(
                    out=st[:, 32:36], in0=st[:, 8:12], in1=st[:, 8:12]),
                    waits=[t_mean], inc="ve")
                t_v2 = sch.op("vector", lambda e: e.tensor_scalar(
                    out=st[:, 12:16], in0=st[:, 4:8], scalar1=1.0 / D,
                    scalar2=None, op0=Alu.mult), waits=[t_sq], inc="ve")
                t_v3 = sch.op("vector", lambda e: e.tensor_sub(
                    out=st[:, 12:16], in0=st[:, 12:16], in1=st[:, 32:36]),
                    waits=[t_v1, t_v2], inc="ve")
                t_v4 = sch.op("vector", lambda e: e.tensor_scalar(
                    out=st[:, 12:16], in0=st[:, 12:16], scalar1=EPS_LN,
                    scalar2=None, op0=Alu.add), waits=[t_v3], inc="ve")
                t_ln = sch.op("scalar", lambda e: e.activation(
                    out=st[:, 32:36], in_=st[:, 12:16], func=Act.Ln),
                    waits=[t_v4], inc="se")
                t_rstd = sch.op("scalar", lambda e: e.activation(
                    out=st[:, 12:16], in_=st[:, 32:36], func=Act.Exp,
                    scale=-0.5), waits=[t_ln], inc="se")
                t_nmr = sch.op("vector", lambda e: e.scalar_tensor_tensor(
                    out=st[:, 16:20], in0=st[:, 8:12], scalar=-1.0,
                    in1=st[:, 12:16], op0=Alu.mult, op1=Alu.mult),
                    waits=[t_rstd], inc="ve")
                t_xn = None
                for tt in range(4):
                    t_xn = sch.op("scalar", lambda e, b=xb, t_=tt: e.activation(
                        out=xn[:, t_, :], in_=b[:, t_, :], func=Act.Identity,
                        bias=st[:, 16 + t_:17 + t_],
                        scale=st[:, 12 + t_:13 + t_]),
                        waits=[t_nmr], inc="se")
                tap("xn", xn[:], [t_xn])

                if stage_a < 2:
                    continue
                t_am = sch.op("vector", lambda e: e.tensor_reduce(
                    out=st[:, 20:24], in_=xn[:], axis=AX, op=Alu.max,
                    apply_absolute_value=True), waits=[t_xn], inc="ve")
                t_c1 = sch.op("vector", lambda e: e.tensor_scalar(
                    out=st[:, 20:24], in0=st[:, 20:24], scalar1=1e-5,
                    scalar2=None, op0=Alu.max), waits=[t_am], inc="ve")
                t_rc = sch.op("vector", lambda e: e.reciprocal(
                    out=st[:, 36:40], in_=st[:, 20:24]), waits=[t_c1], inc="ve")
                t_s1 = sch.op("vector", lambda e: e.tensor_scalar(
                    out=st[:, 24:28], in0=st[:, 36:40], scalar1=127.0,
                    scalar2=None, op0=Alu.mult), waits=[t_rc], inc="ve")
                t_rd1 = sch.op("vector", lambda e: e.tensor_scalar(
                    out=st[:, 28:32], in0=st[:, 20:24], scalar1=1.0 / 127.0,
                    scalar2=None, op0=Alu.mult), waits=[t_c1], inc="ve")
                sch.dma("sync", gs2[0, u], st[:, 28:32], waits=[t_rd1], allow_slow_non_contiguous=True)

                if stage_a < 3:
                    continue
                ev_xnf = []
                for g in range(8):
                    tt_, ka = divmod(4 * g, 8)
                    bnk = g % 2
                    wv = [t_xn, t_pa_free]
                    if g >= 2:
                        wv.append(ev_xnf[g - 2])
                    tmm = None
                    for j in range(4):
                        k_ = ka + j
                        tmm = sch.op("tensor", lambda e, t2=tt_, k2=k_, b2=bnk, j2=j: e.transpose(
                            pa(b2, 128 * j2, 128 * (j2 + 1)),
                            xn[:, t2, 128 * k2:128 * (k2 + 1)], idf[:]),
                            waits=wv, inc="pe")
                    ev_xnf.append(sch.op("scalar", lambda e, t2=tt_, k2=ka, b2=bnk: e.copy(
                        out=xnf[:, k2:k2 + 4, 128 * t2:128 * (t2 + 1)],
                        in_=pa(b2).rearrange("p (j c) -> p j c", c=128)),
                        waits=[tmm], inc="se"))
                t_xnf = ev_xnf[-1]
                tap("xnf", xnf[:], [t_xnf])

                if stage_a < 4:
                    continue
                t_r1 = None
                for tt in range(4):
                    t_r1 = sch.op("vector", lambda e, t_=tt: e.tensor_scalar(
                        out=xn[:, t_, :], in0=xn[:, t_, :],
                        scalar1=st[:, 24 + t_:25 + t_], scalar2=MAGIC,
                        op0=Alu.mult, op1=Alu.add),
                        waits=[t_s1, t_xnf], inc="ve")
                t_r2 = sch.op("vector", lambda e: e.tensor_scalar(
                    out=xq1t[:], in0=xn[:], scalar1=MAGIC, scalar2=None,
                    op0=Alu.subtract), waits=[t_r1], inc="ve")
                tap("xq1t", xq1t[:], [t_r2])

                ev_q1 = []
                for g in range(8):
                    tt_, ka = divmod(4 * g, 8)
                    bnk = 2 + g % 2
                    wv = [t_r2, t_pa_free]
                    if g >= 2:
                        wv.append(ev_q1[g - 2])
                    tmm = None
                    for j in range(4):
                        k_ = ka + j
                        tmm = sch.op("tensor", lambda e, t2=tt_, k2=k_, b2=bnk, j2=j: e.transpose(
                            pab(b2)[:, 128 * j2:128 * (j2 + 1)],
                            xq1t[:, t2, 128 * k2:128 * (k2 + 1)], idb[:]),
                            waits=wv, inc="pe")
                    ev_q1.append(sch.op("scalar", lambda e, t2=tt_, k2=ka, b2=bnk: e.copy(
                        out=q1f[:, k2:k2 + 4, 128 * t2:128 * (t2 + 1)],
                        in_=pab(b2)[:, 0:512].rearrange("p (j c) -> p j c", c=128)),
                        waits=[tmm], inc="se"))
                t_q1dma = sch.dma("sync", xq1s3[:, :, SUB * u:SUB * (u + 1)],
                                  q1f[:], waits=[ev_q1[-1]])

                if stage_a < 5:
                    continue
                ev_proj = []
                tile_idx = 0

                def proj_fm(wt, dst, wv_extra):
                    nonlocal tile_idx
                    toks = []
                    for m in range(8):
                        bnk = tile_idx % 4
                        wv = [t_xnf] + wv_extra
                        if len(ev_proj) >= 4:
                            wv.append(ev_proj[-4])
                        tmm = None
                        for k_ in range(8):
                            tmm = sch.op("tensor", lambda e, m2=m, k2=k_, b2=bnk, w2=wt: e.matmul(
                                pa(b2), w2[:, k2, 128 * m2:128 * (m2 + 1)],
                                xnf[:, k2, :], start=(k2 == 0), stop=(k2 == 7),
                                skip_group_check=True), waits=wv, inc="pe")
                        tev = sch.op("scalar", lambda e, m2=m, b2=bnk, d2=dst: e.copy(
                            out=d2[:, m2, :], in_=pa(b2)), waits=[tmm], inc="se")
                        ev_proj.append(tev)
                        toks.append(tev)
                        tile_idx += 1
                    return toks

                tq = proj_fm(wq_t, qf, [ev_q1[-1]])
                tk = proj_fm(wk_t, kf, [])
                tv = []
                for tt in range(4):
                    for qg in range(2):
                        bnk = tile_idx % 4
                        wv = [t_xnf, t_r2]
                        if len(ev_proj) >= 4:
                            wv.append(ev_proj[-4])
                        tmm = None
                        for k_ in range(8):
                            tmm = sch.op("tensor", lambda e, t2=tt, q2=qg, k2=k_, b2=bnk: e.matmul(
                                pa(b2), xnf[:, k2, 128 * t2:128 * (t2 + 1)],
                                wv_t[:, k2, 512 * q2:512 * (q2 + 1)],
                                start=(k2 == 0), stop=(k2 == 7),
                                skip_group_check=True), waits=wv, inc="pe")
                        tev = sch.op("scalar", lambda e, t2=tt, q2=qg, b2=bnk: e.copy(
                            out=vt[:, t2, 512 * q2:512 * (q2 + 1)], in_=pa(b2)),
                            waits=[tmm], inc="se")
                        ev_proj.append(tev)
                        tv.append(tev)
                        tile_idx += 1
                t_pa_free = ev_proj[-1]
                tap("qf", qf[:], [tq[-1]])
                tap("vt", vt, [tv[-1]])

                if stage_a < 6:
                    continue
                def sigma(src, dst, wv):
                    a = sch.op("vector", lambda e: e.tensor_scalar(
                        out=tmp1, in0=src[:], scalar1=0.0, scalar2=None,
                        op0=Alu.min), waits=wv, inc="ve")
                    b = sch.op("scalar", lambda e: e.activation(
                        out=tmp1, in_=tmp1, func=Act.Exp),
                        waits=[a], inc="se")
                    return sch.op("vector", lambda e: e.scalar_tensor_tensor(
                        out=dst[:], in0=src[:], scalar=0.0, in1=tmp1,
                        op0=Alu.max, op1=Alu.add), waits=[b], inc="ve")

                t_sq_ = sigma(qf, sqf, [tq[-1], t_q1dma])
                t_sk_ = sigma(kf, skf, [tk[-1], t_sq_])
                tap("sqf", sqf[:], [t_sq_])
                tap("skf", skf[:], [t_sk_])

                t_swv = [sch.dma("scalar", vt_sw[0:64], vt[64:128],
                                 waits=[tv[-1], ev_q1[-1], t_segloop_prev,
                                        t_r2]),
                         sch.dma("scalar", vt_sw[64:128], vt[0:64],
                                 waits=[tv[-1], ev_q1[-1], t_segloop_prev,
                                        t_r2])]
                t_swaps = list(t_swv)

                if stage_a < 7:
                    continue
                ev_skt = []
                for g in range(8):
                    tt_, ka = divmod(4 * g, 8)
                    bnk = g % 2
                    wv = [t_sk_, t_pa_free]
                    if g >= 2:
                        wv.append(ev_skt[g - 2])
                    tmm = None
                    for j in range(4):
                        k_ = ka + j
                        tmm = sch.op("tensor", lambda e, t2=tt_, k2=k_, b2=bnk, j2=j: e.transpose(
                            pab(b2)[:, 128 * j2:128 * (j2 + 1)],
                            skf[:, k2, 128 * t2:128 * (t2 + 1)], idb[:]),
                            waits=wv, inc="pe")
                    ev_skt.append(sch.op("scalar", lambda e, t2=tt_, k2=ka, b2=bnk: e.copy(
                        out=skt[:, t2, 512 * (k2 // 4):512 * (k2 // 4 + 1)],
                        in_=pab(b2)[:, 0:512]), waits=[tmm], inc="se"))
                t_skt = ev_skt[-1]
                t_pa_free = t_skt
                tap("skt", skt, [t_skt])
                t_swskt = sch.dma("scalar", skt_sw[0:64], skt[64:128],
                                  waits=[t_skt, t_sk_, t_q1dma])
                t_swaps.append(t_swskt)

                t_zc = sch.op("vector", lambda e: e.tensor_reduce(
                    out=zc[:].rearrange("p (k l) -> p k l", l=8),
                    in_=skf[:].rearrange("p k (l j) -> p k l j", j=64),
                    axis=AX, op=Alu.add), waits=[t_sk_], inc="ve")

                if stage_a < 8:
                    continue
                t_zmc = None
                for ls in range(8):
                    for hp in range(2):
                        t_zmc = sch.op("vector", lambda e, h2=64 * hp, c0=hp, l=ls: e.tensor_copy(
                            out=zmm[h2:h2 + 64, l, c0:c0 + 15:2],
                            in_=zsb[h2:h2 + 64, :]),
                            waits=[t_zc, t_z0, t_den_prev, t_zmm0], inc="ve")
                    t_z0 = sch.op("gpsimd", lambda e, l=ls: e.tensor_tensor(
                        out=zsb[:], in0=zsb[:], in1=zc[:, l:l + 57:8],
                        op=Alu.add), waits=[t_zmc], inc="po")
                t_den = None
                first = True
                for ls in range(8):
                    for side, ssrc in ((0, sqf), (1, skf)):
                        for r in range(8):
                            t_den = sch.op("tensor", lambda e, s3=ssrc, r2=r, o=32 * ls + 16 * side + 2 * r, f=first, l=ls: e.matmul(
                                pac(0, o, o + 2), s3[:, r2, 64 * l:64 * l + 64],
                                zmm[:, l, 2 * r2:2 * r2 + 2], start=f,
                                stop=(o == 254), skip_group_check=True),
                                waits=[t_zmc, t_sq_, t_sk_, t_den_prev,
                                       t_pa_free, t_skt], inc="pe")
                            first = False
                t_dev = sch.op("scalar", lambda e: e.copy(
                    out=denb[0:64].rearrange("p a b -> p (a b)"),
                    in_=pac(0, 0, 256)), waits=[t_den], inc="se")
                t_rd = sch.op("vector", lambda e: e.tensor_scalar(
                    out=denb[0:64].rearrange("p a b -> p (a b)"),
                    in0=denb[0:64].rearrange("p a b -> p (a b)"),
                    scalar1=EPS_ATTN, scalar2=None, op0=Alu.add),
                    waits=[t_dev], inc="ve")
                t_rd = sch.op("vector", lambda e: e.reciprocal(
                    out=denb[0:64].rearrange("p a b -> p (a b)"),
                    in_=denb[0:64].rearrange("p a b -> p (a b)")),
                    waits=[t_rd], inc="ve")
                t_rdk = sch.op("vector", lambda e: e.tensor_scalar(
                    out=denb[0:64, :, 16:32], in0=denb[0:64, :, 16:32],
                    scalar1=-1.0, scalar2=None, op0=Alu.mult),
                    waits=[t_rd], inc="ve")
                t_nq = sch.op("vector", lambda e: e.tensor_tensor(
                    out=nqb[0:64], in0=denb[0:64, :, 0:16],
                    in1=bt[0:64, :].unsqueeze(1).to_broadcast([64, 8, 16]),
                    op=Alu.mult), waits=[t_rd, t_btm], inc="ve")
                t_den_prev = t_den

                t_sc_prev = t_pa_free
                t_exp = t_pa_free
                t_as = [t_pa_free]
                t_t2 = [t_pb_free]
                t_vm1 = t_pb_free
                t_ptr_prev = [t_dev, t_dev]
                t_exp_hist = [t_dev, t_dev]
                t_as_hist = [t_pa_free, t_pa_free]
                for ls in range(max_seg):
                    u2, sp = ls // 2, ls % 2
                    p2 = ls % 2
                    eff = seg_stage if ls < 2 else min(seg_stage, seg2_stage)
                    if eff < 2:
                        continue
                    t_kz = []
                    for hp in range(2):
                        t_kz.append(sch.op("scalar", lambda e, h2=64 * hp, hp2=hp, l=ls: e.copy(
                            out=kz[h2:h2 + 64, hp2:hp2 + 15:2, :],
                            in_=kf[h2:h2 + 64, :, 64 * l:64 * l + 64]),
                            waits=[t_sk_, t_kz0, t_sc_prev], inc="se"))
                    t_sc = None
                    for h in range(16):
                        t_sc = sch.op("tensor", lambda e, h2=h, l=ls, b0=2 * p2: e.matmul(
                            pac(b0 + h2 // 8, 64 * (h2 % 8), 64 * (h2 % 8) + 64),
                            qf[:, h2 // 2, 64 * l:64 * l + 64],
                            kz[:, h2, :],
                            start=(h2 % 8 == 0), stop=(h2 % 8 == 7),
                            skip_group_check=True),
                            waits=[tq[-1], t_dev, t_as_hist[p2],
                                   t_ptr_prev[0], t_ptr_prev[1]] + t_kz,
                            inc="pe")
                    t_sc_prev = t_sc

                    if eff < 3:
                        continue
                    vA = pA[0:64, 1024 * p2:1024 * p2 + 1024].rearrange(
                        "p (s j) -> p s j", j=64)
                    vP = psb[0:64, :].rearrange("p (s j) -> p s j", j=64)
                    t_mx = sch.op("vector", lambda e, v=vA: e.tensor_reduce(
                        out=mxs[0:64, :], in_=v, axis=AX, op=Alu.max),
                        waits=[t_sc], inc="ve")
                    t_sb2 = sch.op("vector", lambda e, v=vA: e.tensor_tensor(
                        out=v, in0=v,
                        in1=mxs[0:64, :].unsqueeze(2).to_broadcast([64, 16, 64]),
                        op=Alu.subtract), waits=[t_mx], inc="ve")
                    t_exp = sch.op("scalar", lambda e, v=vA, w=vP: e.activation(
                        out=w, in_=v, func=Act.Exp, scale=0.125),
                        waits=[t_sb2, t_ptr_prev[0], t_ptr_prev[1]], inc="se")
                    t_exp_hist[p2] = t_exp
                    t_mk = sch.op("vector", lambda e, w=vP: e.tensor_tensor(
                        out=w, in0=w,
                        in1=m01[0:64, :].unsqueeze(1).to_broadcast([64, 16, 64]),
                        op=Alu.mult), waits=[t_exp], inc="ve")
                    t_sm = sch.op("vector", lambda e, w=vP: e.tensor_reduce(
                        out=sms[0:64, :], in_=w, axis=AX, op=Alu.add),
                        waits=[t_mk], inc="ve")
                    t_rs = sch.op("vector", lambda e: e.reciprocal(
                        out=sms[0:64, :], in_=sms[0:64, :]), waits=[t_sm],
                        inc="ve")
                    t_ns = sch.op("vector", lambda e: e.tensor_tensor(
                        out=nsm[0:64, :], in0=sms[0:64, :], in1=btm[0:64, :],
                        op=Alu.mult), waits=[t_rs], inc="ve")
                    t_pn = sch.op("vector", lambda e, w=vP: e.tensor_tensor(
                        out=w, in0=w,
                        in1=nsm[0:64, :].unsqueeze(2).to_broadcast([64, 16, 64]),
                        op=Alu.mult), waits=[t_ns], inc="ve")

                    if eff < 4:
                        continue
                    t_pt = None
                    for h in range(16):
                        t_pt = sch.op("tensor", lambda e, h2=h, b2=2 * (1 - p2): e.transpose(
                            pab(b2)[0:64, 64 * h2:64 * h2 + 64],
                            psb[0:64, 64 * h2:64 * h2 + 64],
                            idb[0:64, 0:64]),
                            waits=[t_pn, t_as_hist[1 - p2], t_exp_hist[1 - p2],
                                   t_ptr_prev[1 - p2]], inc="pe")
                    t_ptr = sch.op("scalar", lambda e, b2=2 * (1 - p2): e.copy(
                        out=ptr[0:64, :], in_=pab(b2)[0:64, 0:1024]),
                        waits=[t_pt, t_ptr0], inc="se")
                    t_ptr_prev[1 - p2] = t_ptr

                    if eff < 5:
                        continue
                    t_ad = None
                    vsel = vt if sp == 0 else vt_sw
                    for h in range(16):
                        t_ad = sch.op("tensor", lambda e, h2=h, v4=vsel, u3=u2, b0=2 * p2: e.matmul(
                            pac(b0 + h2 // 8, 64 * (h2 % 8), 64 * (h2 % 8) + 64),
                            ptr[:, 64 * h2:64 * h2 + 64],
                            v4[:, u3, 64 * h2:64 * h2 + 64],
                            start=(h2 % 8 == 0), stop=(h2 % 8 == 7),
                            skip_group_check=True),
                            waits=[t_ptr, t_exp, t_swv[0], t_swv[1]], inc="pe")

                    if os.environ.get("PBDUMP") and ls == 1:
                        t_ad = sch.op("scalar", lambda e: e.copy(
                            out=outsb[0:64, 1, :], in_=pA[0:64, 1024:2048]),
                            waits=[t_ad], inc="se")
                    if eff < 6:
                        continue
                    t_amem = None
                    for h in range(16):
                        t_amem = sch.op("tensor", lambda e, h2=h, l=ls: e.matmul(
                            pbc(h2 // 8, 64 * (h2 % 8), 64 * (h2 % 8) + 64),
                            sqf[:, h2 // 2, 64 * l:64 * l + 64],
                            msbf[:, h2, :],
                            start=(h2 % 8 == 0), stop=(h2 % 8 == 7),
                            skip_group_check=True),
                            waits=[t_mcast, t_t2[-1]], inc="pe")
                    t_vp = None
                    for h in range(16):
                        t_vp = sch.op("tensor", lambda e, h2=h, l=ls: e.matmul(
                            pbc(2 + h2 // 8, 64 * (h2 % 8), 64 * (h2 % 8) + 64),
                            skf[:, h2 // 2, 64 * l:64 * l + 64],
                            msbf[:, h2, :],
                            start=(h2 % 8 == 0), stop=(h2 % 8 == 7),
                            skip_group_check=True),
                            waits=[t_mcast, t_vm1], inc="pe")

                    if eff < 7:
                        continue
                    if os.environ.get("PBDUMP") and ls == 1:
                        t_amem = sch.op("scalar", lambda e: e.copy(
                            out=outsb[0:64, 0, :], in_=pB[0:64, 0:1024]),
                            waits=[t_amem], inc="se")
                    vc = vt if sp == 0 else vt_sw
                    t_vm1 = sch.op("vector", lambda e, l=ls: e.tensor_tensor(
                        out=vmvp[0:64, :].rearrange("p (s j) -> p s j", j=64),
                        in0=pB[0:64, 1024:2048].rearrange("p (s j) -> p s j", j=64),
                        in1=denb[0:64, l, 16:32].unsqueeze(2).to_broadcast([64, 16, 64]),
                        op=Alu.mult), waits=[t_vp, t_rdk], inc="ve")
                    t_vm2 = sch.op("vector", lambda e, v4=vc, u3=u2: e.tensor_tensor(
                        out=vmvp[0:64, :], in0=vmvp[0:64, :],
                        in1=v4[0:64, u3, :], op=Alu.add),
                        waits=[t_vm1], inc="ve")

                    t_as = [t_as[-1]]
                    t_as_hist[p2] = None
                    for qg in range(2):
                        t2a = sch.op("vector", lambda e, q2=qg, l=ls: e.tensor_tensor(
                            out=tas,
                            in0=pB[0:64, 512 * q2:512 * (q2 + 1)].rearrange("p (s j) -> p s j", j=64),
                            in1=nqb[0:64, l, 8 * q2:8 * q2 + 8].unsqueeze(2).to_broadcast([64, 8, 64]),
                            op=Alu.mult), waits=[t_amem, t_nq, t_as[-1]],
                            inc="ve")
                        t_t2.append(t2a)
                        t_as.append(sch.op("vector", lambda e, q2=qg, l=ls, b0=1024 * p2: e.tensor_tensor(
                            out=ast[0:64, l, 512 * q2:512 * (q2 + 1)],
                            in0=tas, in1=pA[0:64, b0 + 512 * q2:b0 + 512 * (q2 + 1)],
                            op=Alu.add), waits=[t2a, t_ad], inc="ve"))

                    if eff < 8:
                        continue
                    skc = skt if sp == 0 else skt_sw
                    t_dm = None
                    for h in range(16):
                        t_dm = sch.op("tensor", lambda e, h2=h, s4=skc, u3=u2: e.matmul(
                            pB[:, 64 * h2:64 * h2 + 64],
                            s4[:, u3, 128 * (h2 // 2):128 * (h2 // 2) + 128],
                            vmvp[:, 64 * h2:64 * h2 + 64],
                            start=(h2 % 8 == 0), stop=(h2 % 8 == 7),
                            skip_group_check=True),
                            waits=[t_vm2, t_swskt, t_madd] + t_t2[-2:],
                            inc="pe")
                    t_ma0 = sch.op("vector", lambda e: e.tensor_tensor(
                        out=m32[0:64, 0:15:2, :], in0=m32[0:64, 0:15:2, :],
                        in1=pB[0:64, 0:1024].rearrange(
                            "p (s j) -> p s j", j=64)[:, 0:15:2, :],
                        op=Alu.add), waits=[t_dm], inc="ve")
                    t_madd = sch.op("vector", lambda e: e.tensor_tensor(
                        out=m32[64:128, 1:16:2, :], in0=m32[64:128, 1:16:2, :],
                        in1=pB[64:128, 0:1024].rearrange(
                            "p (s j) -> p s j", j=64)[:, 1:16:2, :],
                        op=Alu.add), waits=[t_dm, t_ma0], inc="ve")
                    t_mcast = sch.op("scalar", lambda e: e.copy(
                        out=msbf[:], in_=m32[:]), waits=[t_madd, t_ma0],
                        inc="se")
                    t_as_hist[p2] = t_as[-1]
                t_pa_free = t_as[-1]
                t_segloop_prev = t_madd

                tap("ast", ast[:], [t_as[-1]])
                tap("m32", m32[:], [t_madd])

                if stage_a < 9:
                    continue
                ev_as = []
                for k_ in range(8):
                    wv = [t_as[-1], t_pa_free]
                    if k_ >= 2:
                        wv.append(ev_as[k_ - 2])
                    bnk = k_ % 2
                    tmm = None
                    for l in range(8):
                        tmm = sch.op("tensor", lambda e, k2=k_, l2=l, b2=bnk: e.transpose(
                            pab(b2)[:, 64 * l2:64 * l2 + 64],
                            ast[0:64, l2, 128 * k2:128 * (k2 + 1)],
                            idb[0:64, 0:64]), waits=wv, inc="pe")
                    ev_as.append(sch.op("scalar", lambda e, k2=k_, b2=bnk: e.copy(
                        out=asf[:, k2, :], in_=pab(b2)[:, 0:512]),
                        waits=[tmm], inc="se"))
                t_asf = ev_as[-1]
                tap("asf", asf, [t_asf])

                t_resid = []
                t_odma = {}
                oidx = 0
                for tt in range(4):
                    for qg in range(2):
                        bnk = oidx % 4
                        wv = [t_asf]
                        if len(t_resid) >= 4:
                            wv.append(t_resid[-4])
                        tmm = None
                        for k_ in range(8):
                            tmm = sch.op("tensor", lambda e, t2=tt, q2=qg, k2=k_, b2=bnk: e.matmul(
                                pa(b2), asf[:, k2, 128 * t2:128 * (t2 + 1)],
                                wo_t[:, k2, 512 * q2:512 * (q2 + 1)],
                                start=(k2 == 0), stop=(k2 == 7),
                                skip_group_check=True), waits=wv, inc="pe")
                        t_resid.append(sch.op("vector", lambda e, t2=tt, q2=qg, b2=bnk, xb2=xb: e.tensor_add(
                            out=outsb[:, t2 % 2, 512 * q2:512 * (q2 + 1)],
                            in0=pa(b2), in1=xb2[:, t2, 512 * q2:512 * (q2 + 1)]),
                            waits=[tmm, tx, t_odma.get(tt - 2, t_outdma_prev)],
                            inc="ve"))
                        oidx += 1
                        if qg == 1:
                            t_odma[tt] = sch.dma(
                                "sync", out3[u][:, tt, :],
                                outsb[:, tt % 2, :], waits=[t_resid[-1]])
                tok_xfree[u] = t_resid[-1]
                t_pa_free = t_resid[-1]
                t_outdma_prev = t_odma[3]
                t_out_toks.append(t_outdma_prev)

            bar1 = sch.barrier()

            blkA = eA.enter_context(nc.Block())
            sch.replay(blkA)

        with ExitStack() as eB:
            sbB = lambda n, sh, dt: eB.enter_context(nc.sbuf_tensor(n, sh, dt))
            w1t = sbB("w1t", [128, 8, DH], BF16)
            w2t = sbB("w2t", [128, 32, D], BF16)
            hbuf = sbB("hbuf", [128, 32, 258], F32)
            cvt = sbB("cvt", [128, 32, 128], F32)
            xq2 = sbB("xq2", [128, 32, 128], BF16)
            xq1w = [sbB(f"xq1w{i}", [128, 8, 258], BF16) for i in range(2)]
            grow = [sbB(f"grow{i}", [1, 258], F32) for i in range(2)]
            gsb = sbB("gsb", [128, 258], F32)
            cwsb = sbB("cwsb", [128, 96], F32)
            cbsb = sbB("cbsb", [128, 32], F32)
            rmax = sbB("rmaxb", [128, 128], F32)
            s2s = sbB("s2s", [128, 8], F32)
            s2c = sbB("s2c", [128, 128], F32)
            s2r = sbB("s2r", [1, 128], F32)
            s2b = sbB("s2b", [128, 128], F32)
            outf_s = sbB("outf", [128, D], F32)
            mgc = sbB("mgc", [128, 1], F32)

            t_w1t = sch.dma("sync", w1t[:],
                            w1ts_h.rearrange("(k p) c -> p k c", p=128),
                            waits=[bar1[0]])
            t_w2t = sch.dma("sync", w2t[:],
                            w2ts_h.rearrange("(k p) c -> p k c", p=128),
                            waits=[bar1[0]])
            t_cw = sch.dma("sync", cwsb[:], cwr_h, waits=[bar1[0]])
            t_cb = sch.dma("sync", cbsb[:], cbr_h, waits=[bar1[0]])
            t_mgc = sch.op("gpsimd", lambda e: e.memset(mgc[:], MAGIC),
                           waits=[bar1[0]], inc="po")

            def win_load(w, wv):
                c0 = WIN * w - 1
                lo, hi = max(c0, 0), min(c0 + 258, T)
                j0 = lo - c0
                buf, gbf = xq1w[w % 2], grow[w % 2]
                tks = [sch.dma("scalar", buf[:, :, j0:j0 + hi - lo],
                               xq1s3[:, :, lo:hi], waits=wv),
                       sch.dma("scalar", gbf[:, j0:j0 + hi - lo],
                               gs_h[:, lo:hi], waits=wv)]
                if j0 > 0:
                    tks.append(sch.op("gpsimd", lambda e, b=buf: e.memset(
                        b[:, :, 0:j0], 0.0), waits=wv, inc="po"))
                    tks.append(sch.op("gpsimd", lambda e, g=gbf: e.memset(
                        g[:, 0:j0], 0.0), waits=wv, inc="po"))
                if hi - c0 < 258:
                    tks.append(sch.op("gpsimd", lambda e, b=buf, j1=hi - c0: e.memset(
                        b[:, :, j1:258], 0.0), waits=wv, inc="po"))
                    tks.append(sch.op("gpsimd", lambda e, g=gbf, j1=hi - c0: e.memset(
                        g[:, j1:258], 0.0), waits=wv, inc="po"))
                return tks

            ld = [None] * (NWIN + 1)
            ld[0] = win_load(0, [bar1[0]])
            t_free_xq1w = [bar1[0]] * 2
            t_pb45_free = bar1[0]
            t_pb5_free = bar1[0]
            t_w2o_free = [bar1[0], bar1[0]]
            t_outf_free = bar1[0]
            t_cvt_free = bar1[0]
            t_hb_free = bar1[0]
            t_w2mm_prev = bar1[0]
            out2 = out_h.rearrange("(t p) d -> t p d", p=128)

            for w in range(max_win if do_b else 0):
                buf, gbf = xq1w[w % 2], grow[w % 2]
                if w + 1 < (max_win if do_b else 0):
                    ld[w + 1] = win_load(w + 1, [t_free_xq1w[(w + 1) % 2]])
                lw = ld[w]

                t_gb = sch.op("tensor", lambda e, g=gbf: e.matmul(
                    pb(0, 0, 258), r1row[:], g[:], start=True, stop=True),
                    waits=lw + [t_pb45_free], inc="pe")
                t_gsb = sch.op("scalar", lambda e: e.copy(
                    out=gsb[:], in_=pb(0, 0, 258)), waits=[t_gb, t_hb_free],
                    inc="se")

                ev_h = []
                for g in range(16):
                    bnk0 = 2 * (g % 2)
                    wv = [t_w1t] + lw
                    if g >= 2:
                        wv.append(ev_h[2 * g - 3])
                    tmm = [None, None]
                    for ci in range(2):
                        c = 2 * g + ci
                        for k_ in range(8):
                            tmm[ci] = sch.op("tensor", lambda e, c2=c, k2=k_, b2=bnk0 + ci, bu=buf: e.matmul(
                                pa(b2, 0, 258), w1t[:, k2, 128 * c2:128 * (c2 + 1)],
                                bu[:, k2, :], start=(k2 == 0), stop=(k2 == 7),
                                skip_group_check=True), waits=wv, inc="pe")
                    for ci in range(2):
                        c = 2 * g + ci
                        ev_h.append(sch.op("vector", lambda e, c2=c, b2=bnk0 + ci: e.tensor_tensor(
                            out=hbuf[:, c2, :], in0=pa(b2, 0, 258), in1=gsb[:],
                            op=Alu.mult), waits=[tmm[ci], t_gsb, t_hb_free],
                            inc="ve"))
                t_h = ev_h[-1]
                tap(f"hbuf_{w}", hbuf[:], [t_h])

                for a in range(2):
                    cc = 1 + 128 * a
                    t_ct = []
                    for r in range(32):
                        t1 = sch.op("scalar", lambda e, r2=r, c2=cc: e.activation(
                            out=cvt[:, r2, :], in_=hbuf[:, r2, c2:c2 + 128],
                            func=Act.Identity,
                            scale=cwsb[:, 3 * r2 + 1:3 * r2 + 2],
                            bias=cbsb[:, r2:r2 + 1]),
                            waits=[t_h, t_cw, t_cb, t_cvt_free],
                            inc="se")
                        t2 = sch.op("vector", lambda e, r2=r, c2=cc: e.scalar_tensor_tensor(
                            out=cvt[:, r2, :], in0=hbuf[:, r2, c2 - 1:c2 + 127],
                            scalar=cwsb[:, 3 * r2:3 * r2 + 1], in1=cvt[:, r2, :],
                            op0=Alu.mult, op1=Alu.add), waits=[t1], inc="ve")
                        t_ct.append(sch.op("vector", lambda e, r2=r, c2=cc: e.scalar_tensor_tensor(
                            out=cvt[:, r2, :], in0=hbuf[:, r2, c2 + 1:c2 + 129],
                            scalar=cwsb[:, 3 * r2 + 2:3 * r2 + 3],
                            in1=cvt[:, r2, :], op0=Alu.mult, op1=Alu.add),
                            waits=[t2], inc="ve"))
                    t_si = sch.op("scalar", lambda e: e.activation(
                        out=cvt[:], in_=cvt[:], func=Act.Silu),
                        waits=[t_ct[-1]], inc="se")
                    tap(f"cvt_{w}_{a}", cvt[:], [t_si])
                    t_rm = sch.op("vector", lambda e: e.tensor_reduce(
                        out=rmax[:], in_=cvt[:].transpose([0, 2, 1]), axis=AX,
                        op=Alu.max, apply_absolute_value=True),
                        waits=[t_si], inc="ve")
                    t_tr = sch.op("tensor", lambda e: e.transpose(
                        pb(1, 0, 128), rmax[:], idf[:]),
                        waits=[t_rm, t_pb5_free], inc="pe")
                    t_s2p = sch.op("vector", lambda e: e.tensor_reduce(
                        out=s2s[:, 0:1], in_=pb(1, 0, 128), axis=AX,
                        op=Alu.max), waits=[t_tr], inc="ve")
                    t_cl = sch.op("vector", lambda e: e.tensor_scalar(
                        out=s2s[:, 1:2], in0=s2s[:, 0:1], scalar1=1e-5,
                        scalar2=None, op0=Alu.max), waits=[t_s2p], inc="ve")
                    t_s2 = sch.op("vector", lambda e: e.reciprocal(
                        out=s2s[:, 2:3], in_=s2s[:, 1:2]), waits=[t_cl], inc="ve")
                    t_s2 = sch.op("vector", lambda e: e.tensor_scalar(
                        out=s2s[:, 2:3], in0=s2s[:, 2:3], scalar1=127.0,
                        scalar2=None, op0=Alu.mult), waits=[t_s2], inc="ve")
                    t_dq = sch.op("vector", lambda e: e.tensor_scalar(
                        out=s2s[:, 3:4], in0=s2s[:, 1:2], scalar1=1.0 / 127.0,
                        scalar2=None, op0=Alu.mult), waits=[t_cl], inc="ve")
                    t_dq = sch.op("vector", lambda e: e.tensor_tensor(
                        out=s2s[:, 4:5], in0=s2s[:, 3:4], in1=prs[:, 2:3],
                        op=Alu.mult), waits=[t_dq], inc="ve")
                    t_cp = sch.op("vector", lambda e: e.tensor_copy(
                        out=s2c[:, 0:1], in_=s2s[:, 2:3]), waits=[t_s2], inc="ve")
                    t_tr2 = sch.op("tensor", lambda e: e.transpose(
                        pb(1, 128, 256), s2c[:], idf[:]),
                        waits=[t_cp, t_s2p], inc="pe")
                    t_er = sch.op("scalar", lambda e: e.copy(
                        out=s2r[:], in_=pB[0:1, 512 * 1 + 128:512 * 1 + 256]),
                        waits=[t_tr2], inc="se")
                    t_pb5_free = t_er
                    t_bb = sch.op("tensor", lambda e: e.matmul(
                        pb(0, 258, 386), onesf[:], s2r[:], start=True,
                        stop=True, skip_group_check=True),
                        waits=[t_er, t_gb], inc="pe")
                    t_bs = sch.op("scalar", lambda e: e.copy(
                        out=s2b[:], in_=pb(0, 258, 386)), waits=[t_bb], inc="se")
                    t_m2 = sch.op("gpsimd", lambda e: e.tensor_tensor(
                        out=cvt[:], in0=cvt[:],
                        in1=s2b[:].unsqueeze(1).to_broadcast([128, 32, 128]),
                        op=Alu.mult), waits=[t_bs, t_rm], inc="po")
                    t_q2a = sch.op("scalar", lambda e: e.activation(
                        out=cvt[:], in_=cvt[:], func=Act.Identity,
                        bias=mgc[:, 0:1]), waits=[t_m2, t_mgc], inc="se")
                    t_q2 = sch.op("vector", lambda e: e.tensor_scalar(
                        out=xq2[:], in0=cvt[:], scalar1=MAGIC, scalar2=None,
                        op0=Alu.subtract),
                        waits=[t_q2a, t_w2mm_prev], inc="ve")
                    t_cvt_free = t_q2
                    tap(f"xq2_{w}_{a}", xq2[:], [t_q2])
                    t_ev2 = []
                    for qg in range(2):
                        tmm = None
                        for r in range(32):
                            tmm = sch.op("tensor", lambda e, r2=r, q2=qg: e.matmul(
                                pb(2 + q2), xq2[:, r2, :],
                                w2t[:, r2, 512 * q2:512 * (q2 + 1)],
                                start=(r2 == 0), stop=(r2 == 31),
                                skip_group_check=True),
                                waits=[t_q2, t_w2t, t_w2o_free[qg]], inc="pe")
                        t_ev2.append(sch.op("scalar", lambda e, q2=qg: e.activation(
                            out=outf_s[:, 512 * q2:512 * (q2 + 1)], in_=pb(2 + q2),
                            func=Act.Identity, scale=s2s[:, 4:5], bias=0.0),
                            waits=[tmm, t_dq, t_outf_free], inc="se"))
                        t_w2o_free[qg] = t_ev2[-1]
                    t_w2mm_prev = t_ev2[-1]
                    t_outf_free = sch.dma(
                        "gpsimd", out2[2 * w + a], outf_s[:], waits=t_ev2,
                        accum_op=Alu.add)
                t_hb_free = t_ct[-1]
                t_free_xq1w[w % 2] = ev_h[-1]
                t_pb45_free = t_bs

            sch.op("gpsimd", lambda e: e.nop(), waits=[("dgp", sch.cnt["dgp"])])
            sch.op("sync", lambda e: e.nop(), waits=[("dsy", sch.cnt["dsy"])])
            sch.op("scalar", lambda e: e.nop(), waits=[("dsc", sch.cnt["dsc"])])
            sch.barrier()

            blkB = eB.enter_context(nc.Block())
            sch.replay(blkB)

    return nc


_NC_CACHE = {}


def _get_nc(key, debug_tap=None):
    if key not in _NC_CACHE:
        _NC_CACHE[key] = build_nc(debug_tap)
    return _NC_CACHE[key]


def make_in_maps(inputs):
    x = np.ascontiguousarray(np.asarray(inputs["L0_attn_x"], dtype=np.float32))
    wqT = np.ascontiguousarray(np.asarray(inputs["L0_attn_W_q"], np.float32).T)
    wkT = np.ascontiguousarray(np.asarray(inputs["L0_attn_W_k"], np.float32).T)
    wvT = np.ascontiguousarray(np.asarray(inputs["L0_attn_W_v"], np.float32).T)
    woT = np.ascontiguousarray(np.asarray(inputs["L0_attn_W_o"], np.float32).T)
    beta = np.asarray(inputs["L0_attn_beta_param"], np.float32).reshape(1, H)
    w1T = np.ascontiguousarray(np.asarray(inputs["L0_ffn_w1"], np.float32).T)
    w2T = np.ascontiguousarray(np.asarray(inputs["L0_ffn_w2"], np.float32).T)
    cwr = np.ascontiguousarray(
        np.asarray(inputs["L0_ffn_conv_w"], np.float32).reshape(32, 128, 3)
        .transpose(1, 0, 2).reshape(128, 96))
    cbr = np.ascontiguousarray(
        np.asarray(inputs["L0_ffn_conv_b"], np.float32).reshape(32, 128).T)
    maps = []
    for b in range(8):
        maps.append({"x": np.ascontiguousarray(x[b]), "wqT": wqT, "wkT": wkT,
                     "wvT": wvT, "woT": woT, "beta": beta, "w1T": w1T,
                     "cwr": cwr, "cbr": cbr, "w2T": w2T})
    return maps


def kernel(**inputs):
    nc = _get_nc("main")
    maps = make_in_maps(inputs)
    res = run_bass_kernel_spmd(nc, maps, core_ids=list(range(8)))
    return np.stack([np.asarray(r["out"], dtype=np.float32)
                     for r in res.results], axis=0)


# version 07 (final)
# speedup vs baseline: 1.0732x; 1.0732x over previous
import os
import numpy as np
from contextlib import ExitStack

import concourse.bass as bass
import concourse.mybir as mybir
from concourse.bass_utils import run_bass_kernel_spmd

F32 = mybir.dt.float32
BF16 = mybir.dt.bfloat16
Alu = mybir.AluOpType
Act = mybir.ActivationFunctionType
AX = mybir.AxisListType.X

MAGIC = float(np.float32(1.5 * 2 ** 23))
T, D, DH, H = 4096, 1024, 4096, 16
SUB, NSUB = 512, 8
WIN, NWIN = 256, 16
EPS_ATTN = 1e-6
EPS_LN = 1e-5


class Sch:
    ENGINES = ("sync", "tensor", "vector", "scalar", "gpsimd")

    def __init__(self, nc):
        self.nc = nc
        self.q = {e: [] for e in self.ENGINES}
        self.sem = {}
        self.cnt = {}
        self.waited = {e: {} for e in self.ENGINES}

    def add_sem(self, name, handle):
        self.sem[name] = handle
        self.cnt[name] = 0

    def op(self, eng, fn, waits=(), inc=None, amt=1):
        real = []
        for wv in waits:
            if wv is None:
                continue
            s, v = wv
            w = self.waited[eng]
            if v > w.get(s, 0):
                w[s] = v
                real.append((s, v))
        tok = None
        if inc is not None:
            self.cnt[inc] += amt
            tok = (inc, self.cnt[inc])
        self.q[eng].append((real, fn, tok, amt))
        return tok

    def dma(self, eng, out, in_, waits=(), **kw):
        sem = {"sync": "dsy", "scalar": "dsc", "gpsimd": "dgp"}[eng]
        waits = list(waits)
        waits.append((sem, self.cnt[sem]))
        return self.op(eng, lambda e: e.dma_start(out=out, in_=in_, **kw),
                       waits, inc=sem, amt=16)

    def replay(self, block):
        for eng in self.ENGINES:
            ops = list(self.q[eng])
            if not ops:
                continue

            def body(e, ops_local=ops):
                for waits, fn, tok, amt in ops_local:
                    for s, v in waits:
                        e.wait_ge(self.sem[s], v)
                    inst = fn(e)
                    if tok is not None:
                        inst.then_inc(self.sem[tok[0]], amt)
            getattr(block, eng)(body)
            self.q[eng] = []

    def barrier(self):
        toks = []
        for eng, sem in [("sync", "sy"), ("tensor", "pe"), ("vector", "ve"),
                         ("scalar", "se"), ("gpsimd", "po")]:
            toks.append(self.op(eng, lambda e: e.drain(), inc=sem))
        for s in ("dsy", "dsc", "dgp"):
            toks.append((s, self.cnt[s]))
        for eng in self.ENGINES:
            self.op(eng, lambda e: e.nop(), waits=toks)
        return toks


def build_nc(debug_tap=None, max_sub=NSUB, do_b=True, max_win=NWIN, stage_a=99, max_seg=8, seg_stage=99, seg2_stage=99):
    nc = bass.Bass("TRN2", target_bir_lowering=False)

    x_h = nc.dram_tensor("x", [T, D], F32, kind="ExternalInput").ap()
    wqT_h = nc.dram_tensor("wqT", [D, D], F32, kind="ExternalInput").ap()
    wkT_h = nc.dram_tensor("wkT", [D, D], F32, kind="ExternalInput").ap()
    wvT_h = nc.dram_tensor("wvT", [D, D], F32, kind="ExternalInput").ap()
    woT_h = nc.dram_tensor("woT", [D, D], F32, kind="ExternalInput").ap()
    beta_h = nc.dram_tensor("beta", [1, H], F32, kind="ExternalInput").ap()
    w1T_h = nc.dram_tensor("w1T", [D, DH], F32, kind="ExternalInput").ap()
    cwr_h = nc.dram_tensor("cwr", [128, 96], F32, kind="ExternalInput").ap()
    cbr_h = nc.dram_tensor("cbr", [128, 32], F32, kind="ExternalInput").ap()
    w2T_h = nc.dram_tensor("w2T", [DH, D], F32, kind="ExternalInput").ap()
    out_h = nc.dram_tensor("out", [T, D], F32, kind="ExternalOutput").ap()

    xq1s_h = nc.dram_tensor("xq1s", [D, T], BF16, kind="Internal").ap()
    w1ts_h = nc.dram_tensor("w1ts", [D, DH], BF16, kind="Internal").ap()
    w2ts_h = nc.dram_tensor("w2ts", [DH, D], BF16, kind="Internal").ap()
    gs_h = nc.dram_tensor("gs", [1, T], F32, kind="Internal").ap()

    dbg_h = None
    if debug_tap is not None:
        dbg_h = nc.dram_tensor("dbg", list(debug_tap[1]), debug_tap[2],
                               kind="ExternalOutput").ap()

    es = ExitStack()
    sb = lambda n, sh, dt: es.enter_context(nc.sbuf_tensor(n, sh, dt))

    sch = Sch(nc)
    for s in ("dsy", "dsc", "dgp", "pe", "ve", "se", "po", "sy"):
        sch.add_sem(s, es.enter_context(nc.semaphore(s)))

    pA = es.enter_context(nc.psum_tensor("pA", [128, 2048], F32))
    pB = es.enter_context(nc.psum_tensor("pB", [128, 2048], F32))
    idb = sb("idb", [128, 128], BF16)
    idf = sb("idf", [128, 128], F32)
    m01 = sb("m01", [128, 64], BF16)
    m01f = sb("m01f", [128, 64], F32)
    bt = sb("bt", [128, 16], F32)
    btm = sb("btm", [128, 16], F32)
    prs = sb("prs", [128, 8], F32)
    onesf = sb("onesf", [1, 128], F32)
    r1row = sb("r1row", [1, 128], F32)

    taps = {}

    def tap(name, ap, waits):
        if debug_tap is not None and debug_tap[0] == name:
            taps[name] = sch.dma("sync", dbg_h, ap, waits=waits)

    def pa(b, c0=0, c1=512):
        return pA[:, 512 * b + c0:512 * b + c1]

    def pb(b, c0=0, c1=512):
        return pB[:, 512 * b + c0:512 * b + c1]

    def pac(b, c0=0, c1=512):
        return pA[0:64, 512 * b + c0:512 * b + c1]

    def pbc(b, c0=0, c1=512):
        return pB[0:64, 512 * b + c0:512 * b + c1]

    def pab(b):
        return pA[:, 512 * b:512 * (b + 1)].bitcast(BF16)

    def pbb(b):
        return pB[:, 512 * b:512 * (b + 1)].bitcast(BF16)

    TP = lambda hp: (64 * hp, 0) if hp else None

    with es:
        with ExitStack() as eA:
            sbA = lambda n, sh, dt: eA.enter_context(nc.sbuf_tensor(n, sh, dt))
            wq_t = sbA("wq_t", [128, 8, D], BF16)
            wk_t = sbA("wk_t", [128, 8, D], BF16)
            wv_t = sbA("wv_t", [128, 8, D], BF16)
            wo_t = sbA("wo_t", [128, 8, D], BF16)
            eP = ExitStack()
            sbP = lambda n, sh, dt: eP.enter_context(nc.sbuf_tensor(n, sh, dt))
            bsm = sbP("bsm", [1, 16], F32)
            wsl = [sbP(f"wsl{i}", [128, DH], F32) for i in range(2)]
            wsb = [sbP(f"wsb{i}", [128, DH], BF16) for i in range(2)]
            slr = sbP("slr", [128, 40], F32)
            tot = sbP("tot", [1, 8], F32)

            sch.op("gpsimd", lambda e: e.memset(onesf[:], 1.0))
            tm1 = sch.op("gpsimd", lambda e: e.memset(idb[:], 0.0), inc="po")
            tm2 = sch.op("gpsimd", lambda e: e.memset(idf[:], 0.0), inc="po")
            tm3 = sch.op("gpsimd", lambda e: e.memset(m01f[:], 1.0), inc="po")
            sch.op("gpsimd", lambda e: e.affine_select(
                out=idb[:], in_=idb[:], compare_op=Alu.not_equal, fill=1.0,
                base=0, pattern=[[-1, 128]], channel_multiplier=1),
                waits=[tm1], inc="po")
            sch.op("gpsimd", lambda e: e.affine_select(
                out=idf[:], in_=idf[:], compare_op=Alu.not_equal, fill=1.0,
                base=0, pattern=[[-1, 128]], channel_multiplier=1),
                waits=[tm2], inc="po")
            t_m01f = tm3
            for half in range(2):
                t_m01f = sch.op("gpsimd", lambda e, h0=64 * half: e.affine_select(
                    out=m01f[h0:h0 + 64, :], in_=m01f[h0:h0 + 64, :],
                    compare_op=Alu.is_ge, fill=0.0, base=0,
                    pattern=[[-1, 64]], channel_multiplier=1),
                    waits=[t_m01f], inc="po")
            sch.op("vector", lambda e: e.tensor_copy(out=m01[:], in_=m01f[:]),
                   waits=[t_m01f], inc="ve")
            t_ones = sch.op("vector", lambda e: e.memset(slr[:, 39:40], 1.0),
                            inc="ve")
            t_wsb0 = sch.op("vector", lambda e: e.memset(wsb[0][:], 0.0), inc="ve")
            t_wsb1 = sch.op("vector", lambda e: e.memset(wsb[1][:], 0.0), inc="ve")
            t_beta = sch.dma("gpsimd", bsm[:], beta_h)
            t_sig = sch.op("scalar", lambda e: e.activation(
                out=bsm[:], in_=bsm[:], func=Act.Sigmoid), waits=[t_beta],
                inc="se")
            t_btmm = sch.op("tensor", lambda e: e.matmul(
                pA[:, 512:528], onesf[:], bsm[:], start=True, stop=True),
                waits=[t_sig, t_ones], inc="pe")
            t_bt = sch.op("scalar", lambda e: e.copy(
                out=bt[:], in_=pA[:, 512:528]), waits=[t_btmm], inc="se")
            t_btm = sch.op("vector", lambda e: e.tensor_scalar(
                out=btm[:], in0=bt[:], scalar1=-1.0, scalar2=1.0,
                op0=Alu.mult, op1=Alu.add), waits=[t_bt], inc="ve")
            t_prs_prev = t_bt
            for w_h, n_sl, wts_h, sc in ((w1T_h, 8, w1ts_h, 0), (w2T_h, 32, w2ts_h, 2)):
                ncols = w_h.shape[1]
                w3 = w_h.rearrange("(k p) c -> p k c", p=128)
                wts3 = wts_h.rearrange("(k p) c -> p k c", p=128)
                t_vsnap = ("ve", sch.cnt["ve"])
                t_dsnap = ("dsy", sch.cnt["dsy"])
                reds = []
                for s in range(n_sl):
                    buf = wsl[s % 2]
                    wv = [reds[s - 2]] if s >= 2 else [t_vsnap]
                    tl = sch.dma("scalar", buf[:, 0:ncols], w3[:, s, :], waits=wv)
                    reds.append(sch.op("vector", lambda e, b=buf, s_=s, nc2=ncols: e.tensor_reduce(
                        out=slr[:, s_:s_ + 1], in_=b[:, 0:nc2], axis=AX,
                        op=Alu.add, apply_absolute_value=True),
                        waits=[tl], inc="ve"))
                tsum = sch.op("vector", lambda e, n=n_sl: e.tensor_reduce(
                    out=slr[:, 38:39], in_=slr[:, 0:n], axis=AX, op=Alu.add),
                    waits=[reds[-1]], inc="ve")
                tmm = sch.op("tensor", lambda e: e.matmul(
                    pA[0:1, 0:1], slr[:, 38:39], slr[:, 39:40],
                    start=True, stop=True), waits=[tsum, t_ones], inc="pe")
                tcp = sch.op("scalar", lambda e: e.copy(
                    out=tot[0:1, 0:1], in_=pA[0:1, 0:1]), waits=[tmm], inc="se")
                t2 = sch.op("vector", lambda e, nc_=ncols, n=n_sl: e.tensor_scalar(
                    out=tot[0:1, 1:2], in0=tot[0:1, 0:1],
                    scalar1=1.0 / (nc_ * n * 128), scalar2=1e-5,
                    op0=Alu.mult, op1=Alu.max), waits=[tcp], inc="ve")
                t3 = sch.op("vector", lambda e: e.reciprocal(
                    out=tot[0:1, 2:3], in_=tot[0:1, 1:2]), waits=[t2], inc="ve")
                t4mm = sch.op("tensor", lambda e: e.matmul(
                    pA[:, 528:530], onesf[:], tot[0:1, 1:3], start=True,
                    stop=True, skip_group_check=True),
                    waits=[t3, t_prs_prev], inc="pe")
                t4 = sch.op("scalar", lambda e, c=sc: e.copy(
                    out=prs[:, c:c + 2], in_=pA[:, 528:530]), waits=[t4mm],
                    inc="se")
                t_prs_prev = t4
                st_toks = [None, None]
                for s in range(n_sl):
                    buf = wsl[s % 2]
                    obf = wsb[s % 2]
                    wv = [t4, reds[-1]]
                    if s >= 2:
                        wv.append(st_toks[s % 2])
                    tl = sch.dma("scalar", buf[:, 0:ncols], w3[:, s, :], waits=wv)
                    tr1 = sch.op("vector", lambda e, b=buf, c=sc, nc2=ncols: e.tensor_scalar(
                        out=b[:, 0:nc2], in0=b[:, 0:nc2],
                        scalar1=prs[:, c + 1:c + 2], scalar2=MAGIC,
                        op0=Alu.mult, op1=Alu.add), waits=[tl], inc="ve")
                    tr2 = sch.op("vector", lambda e, b=buf, nc2=ncols: e.tensor_scalar(
                        out=b[:, 0:nc2], in0=b[:, 0:nc2], scalar1=MAGIC,
                        scalar2=1.0, op0=Alu.subtract, op1=Alu.min),
                        waits=[tr1], inc="ve")
                    tr3 = sch.op("vector", lambda e, b=buf, o=obf, nc2=ncols: e.tensor_scalar(
                        out=o[:, 0:nc2], in0=b[:, 0:nc2], scalar1=-1.0,
                        scalar2=None, op0=Alu.max),
                        waits=[tr2, t_wsb0, t_wsb1, t_dsnap, st_toks[s % 2]],
                        inc="ve")
                    st_toks[s % 2] = sch.dma("sync", wts3[:, s, :],
                                             obf[:, 0:ncols], waits=[tr3])

            for w_h, wsb_ in ((wqT_h, wq_t), (wkT_h, wk_t), (wvT_h, wv_t),
                              (woT_h, wo_t)):
                sch.dma("gpsimd", wsb_[:], w_h.rearrange("(k p) c -> p k c", p=128))

            bar0 = sch.barrier()
            sch.op("vector", lambda e: e.tensor_scalar(
                out=r1row[:], in0=onesf[:], scalar1=prs[0:1, 0:1], scalar2=None,
                op0=Alu.mult), waits=[bar0[0]], inc="ve")
            with eP:
                blkP = eP.enter_context(nc.Block())
                sch.replay(blkP)

            xa = sbA("xa0", [128, 4, D], F32)
            xn = sbA("xn", [128, 4, D], F32)
            st = sbA("st", [128, 40], F32)
            xq1t = sbA("xq1t", [128, 4, D], BF16)
            xnf = sbA("xnf", [128, 8, SUB], BF16)
            q1f = sbA("q1f", [128, 8, SUB], BF16)
            qf = sbA("qf", [128, 8, SUB], BF16)
            kf = sbA("kf", [128, 8, SUB], BF16)
            vt = xn[:].bitcast(BF16)[:, :, 0:D]
            sqf = sbA("sqf", [128, 8, SUB], BF16)
            skf = sbA("skf", [128, 8, SUB], BF16)
            kz = sbA("kz", [128, 16, 64], BF16)
            skt = xnf[:].rearrange("p k t -> p (k t)").rearrange(
                "p (a b) -> p a b", a=4)
            tmp1 = q1f[:]
            skt_sw = q1f[:].rearrange("p k t -> p (k t)").rearrange(
                "p (a b) -> p a b", a=4)
            vt_sw = xq1t[:]
            asf = xq1t[:].rearrange("p a b -> p (a b)").rearrange(
                "p (k t) -> p k t", k=8)
            psb = sbA("psb", [128, 1024], BF16)
            ptr = sbA("ptr", [128, 1024], BF16)
            vmvp = sbA("vmvp", [128, 1024], BF16)
            m32 = sbA("m32", [128, 16, 64], F32)
            msbf = sbA("msbf", [128, 16, 64], BF16)
            ast = sbA("ast", [128, 8, D], BF16)
            junk = ast[:].rearrange("p s d -> p (s d)")[:, 0:2048].bitcast(F32)
            outsb = sbA("outsb", [128, 2, D], F32)
            zsb = sbA("zsb", [128, 8], F32)
            zc = sbA("zc", [128, 64], F32)
            zmm = sbA("zmm", [128, 8, 16], BF16)
            denb = sbA("denb", [128, 8, 32], F32)
            nqb = sbA("nqb", [128, 8, 16], F32)
            mxs = sbA("mxs", [128, 16], F32)
            sms = sbA("sms", [128, 16], F32)
            nsm = sbA("nsm", [128, 16], BF16)
            tasb = sbA("tasb", [128, 512], F32)
            tas = tasb[0:64, :]

            t_z0 = sch.op("gpsimd", lambda e: e.memset(zsb[:], 0.0),
                          waits=[bar0[0]], inc="po")
            t_kz0 = sch.op("gpsimd", lambda e: e.memset(kz[:], 0.0),
                           waits=[bar0[0]], inc="po")
            t_zmm0 = sch.op("gpsimd", lambda e: e.memset(zmm[:], 0.0),
                            waits=[bar0[0]], inc="po")
            t_ptr0 = sch.op("gpsimd", lambda e: e.memset(ptr[64:128, :], 0.0),
                            waits=[bar0[0]], inc="po")
            t_vm0 = sch.op("gpsimd", lambda e: e.memset(vmvp[64:128, :], 0.0),
                           waits=[bar0[0]], inc="po")
            t_madd = sch.op("vector", lambda e: e.memset(m32[:], 0.0),
                            waits=[bar0[0]], inc="ve")
            t_mcast = sch.op("scalar", lambda e: e.copy(
                out=msbf[:], in_=m32[:]), waits=[t_madd], inc="se")

            x3 = x_h.rearrange("(s t p) d -> s p t d", p=128, t=4)
            out3 = out_h.rearrange("(s t p) d -> s p t d", p=128, t=4)
            xq1s3 = xq1s_h.rearrange("(k p) t -> p k t", p=128)
            gs2 = gs_h.rearrange("o (s t p) -> o s p t", p=128, t=4)

            tok_xfree = [None] * NSUB
            t_pa_free = bar0[0]
            t_pb_free = bar0[0]
            t_out_toks = []
            t_outdma_prev = bar0[0]
            t_segloop_prev = bar0[0]
            t_den_prev = bar0[0]

            for u in range(max_sub):
                xb = xa
                wv = [tok_xfree[u - 1]] if u >= 1 else [bar0[0]]
                tx = sch.dma("sync", xa[:], x3[u], waits=wv)

                t_sum = sch.op("vector", lambda e, b=xb: e.tensor_reduce(
                    out=st[:, 0:4], in_=b[:], axis=AX, op=Alu.add),
                    waits=[tx], inc="ve")
                t_sq = None
                for tt in range(4):
                    t_sq = sch.op("scalar", lambda e, b=xb, t_=tt: e.activation(
                        out=junk[:], in_=b[:, t_, :], func=Act.Square,
                        accum_out=st[:, 4 + t_:5 + t_]),
                        waits=[tx] + ([t_sq] if t_sq else []), inc="se")
                t_mean = sch.op("vector", lambda e: e.tensor_scalar(
                    out=st[:, 8:12], in0=st[:, 0:4], scalar1=1.0 / D,
                    scalar2=None, op0=Alu.mult), waits=[t_sum], inc="ve")
                t_v1 = sch.op("vector", lambda e: e.tensor_mul(
                    out=st[:, 32:36], in0=st[:, 8:12], in1=st[:, 8:12]),
                    waits=[t_mean], inc="ve")
                t_v2 = sch.op("vector", lambda e: e.tensor_scalar(
                    out=st[:, 12:16], in0=st[:, 4:8], scalar1=1.0 / D,
                    scalar2=None, op0=Alu.mult), waits=[t_sq], inc="ve")
                t_v3 = sch.op("vector", lambda e: e.tensor_sub(
                    out=st[:, 12:16], in0=st[:, 12:16], in1=st[:, 32:36]),
                    waits=[t_v1, t_v2], inc="ve")
                t_v4 = sch.op("vector", lambda e: e.tensor_scalar(
                    out=st[:, 12:16], in0=st[:, 12:16], scalar1=EPS_LN,
                    scalar2=None, op0=Alu.add), waits=[t_v3], inc="ve")
                t_ln = sch.op("scalar", lambda e: e.activation(
                    out=st[:, 32:36], in_=st[:, 12:16], func=Act.Ln),
                    waits=[t_v4], inc="se")
                t_rstd = sch.op("scalar", lambda e: e.activation(
                    out=st[:, 12:16], in_=st[:, 32:36], func=Act.Exp,
                    scale=-0.5), waits=[t_ln], inc="se")
                t_nmr = sch.op("vector", lambda e: e.scalar_tensor_tensor(
                    out=st[:, 16:20], in0=st[:, 8:12], scalar=-1.0,
                    in1=st[:, 12:16], op0=Alu.mult, op1=Alu.mult),
                    waits=[t_rstd], inc="ve")
                t_xn = None
                for tt in range(4):
                    t_xn = sch.op("scalar", lambda e, b=xb, t_=tt: e.activation(
                        out=xn[:, t_, :], in_=b[:, t_, :], func=Act.Identity,
                        bias=st[:, 16 + t_:17 + t_],
                        scale=st[:, 12 + t_:13 + t_]),
                        waits=[t_nmr], inc="se")
                tap("xn", xn[:], [t_xn])

                if stage_a < 2:
                    continue
                t_am = sch.op("vector", lambda e: e.tensor_reduce(
                    out=st[:, 20:24], in_=xn[:], axis=AX, op=Alu.max,
                    apply_absolute_value=True), waits=[t_xn], inc="ve")
                t_c1 = sch.op("vector", lambda e: e.tensor_scalar(
                    out=st[:, 20:24], in0=st[:, 20:24], scalar1=1e-5,
                    scalar2=None, op0=Alu.max), waits=[t_am], inc="ve")
                t_rc = sch.op("vector", lambda e: e.reciprocal(
                    out=st[:, 36:40], in_=st[:, 20:24]), waits=[t_c1], inc="ve")
                t_s1 = sch.op("vector", lambda e: e.tensor_scalar(
                    out=st[:, 24:28], in0=st[:, 36:40], scalar1=127.0,
                    scalar2=None, op0=Alu.mult), waits=[t_rc], inc="ve")
                t_rd1 = sch.op("vector", lambda e: e.tensor_scalar(
                    out=st[:, 28:32], in0=st[:, 20:24], scalar1=1.0 / 127.0,
                    scalar2=None, op0=Alu.mult), waits=[t_c1], inc="ve")
                sch.dma("sync", gs2[0, u], st[:, 28:32], waits=[t_rd1], allow_slow_non_contiguous=True)

                if stage_a < 3:
                    continue
                ev_xnf = []
                for g in range(8):
                    tt_, ka = divmod(4 * g, 8)
                    bnk = g % 2
                    wv = [t_xn, t_pa_free]
                    if g >= 2:
                        wv.append(ev_xnf[g - 2])
                    tmm = None
                    for j in range(4):
                        k_ = ka + j
                        tmm = sch.op("tensor", lambda e, t2=tt_, k2=k_, b2=bnk, j2=j: e.transpose(
                            pa(b2, 128 * j2, 128 * (j2 + 1)),
                            xn[:, t2, 128 * k2:128 * (k2 + 1)], idf[:]),
                            waits=wv, inc="pe")
                    ev_xnf.append(sch.op("scalar", lambda e, t2=tt_, k2=ka, b2=bnk: e.copy(
                        out=xnf[:, k2:k2 + 4, 128 * t2:128 * (t2 + 1)],
                        in_=pa(b2).rearrange("p (j c) -> p j c", c=128)),
                        waits=[tmm], inc="se"))
                t_xnf = ev_xnf[-1]
                tap("xnf", xnf[:], [t_xnf])

                if stage_a < 4:
                    continue
                t_r1 = None
                for tt in range(4):
                    t_r1 = sch.op("vector", lambda e, t_=tt: e.tensor_scalar(
                        out=xn[:, t_, :], in0=xn[:, t_, :],
                        scalar1=st[:, 24 + t_:25 + t_], scalar2=MAGIC,
                        op0=Alu.mult, op1=Alu.add),
                        waits=[t_s1, t_xnf], inc="ve")
                t_r2 = sch.op("vector", lambda e: e.tensor_scalar(
                    out=xq1t[:], in0=xn[:], scalar1=MAGIC, scalar2=None,
                    op0=Alu.subtract), waits=[t_r1], inc="ve")
                tap("xq1t", xq1t[:], [t_r2])

                ev_q1 = []
                for g in range(8):
                    tt_, ka = divmod(4 * g, 8)
                    bnk = 2 + g % 2
                    wv = [t_r2, t_pa_free]
                    if g >= 2:
                        wv.append(ev_q1[g - 2])
                    tmm = None
                    for j in range(4):
                        k_ = ka + j
                        tmm = sch.op("tensor", lambda e, t2=tt_, k2=k_, b2=bnk, j2=j: e.transpose(
                            pab(b2)[:, 128 * j2:128 * (j2 + 1)],
                            xq1t[:, t2, 128 * k2:128 * (k2 + 1)], idb[:]),
                            waits=wv, inc="pe")
                    ev_q1.append(sch.op("scalar", lambda e, t2=tt_, k2=ka, b2=bnk: e.copy(
                        out=q1f[:, k2:k2 + 4, 128 * t2:128 * (t2 + 1)],
                        in_=pab(b2)[:, 0:512].rearrange("p (j c) -> p j c", c=128)),
                        waits=[tmm], inc="se"))
                t_q1dma = sch.dma("sync", xq1s3[:, :, SUB * u:SUB * (u + 1)],
                                  q1f[:], waits=[ev_q1[-1]])

                if stage_a < 5:
                    continue
                ev_proj = []
                tile_idx = 0

                def proj_fm(wt, dst, wv_extra):
                    nonlocal tile_idx
                    toks = []
                    for m in range(8):
                        bnk = tile_idx % 4
                        wv = [t_xnf] + wv_extra
                        if len(ev_proj) >= 4:
                            wv.append(ev_proj[-4])
                        tmm = None
                        for k_ in range(8):
                            tmm = sch.op("tensor", lambda e, m2=m, k2=k_, b2=bnk, w2=wt: e.matmul(
                                pa(b2), w2[:, k2, 128 * m2:128 * (m2 + 1)],
                                xnf[:, k2, :], start=(k2 == 0), stop=(k2 == 7),
                                skip_group_check=True), waits=wv, inc="pe")
                        tev = sch.op("scalar", lambda e, m2=m, b2=bnk, d2=dst: e.copy(
                            out=d2[:, m2, :], in_=pa(b2)), waits=[tmm], inc="se")
                        ev_proj.append(tev)
                        toks.append(tev)
                        tile_idx += 1
                    return toks

                tq = proj_fm(wq_t, qf, [ev_q1[-1]])
                tk = proj_fm(wk_t, kf, [])
                tv = []
                for tt in range(4):
                    for qg in range(2):
                        bnk = tile_idx % 4
                        wv = [t_xnf, t_r2]
                        if len(ev_proj) >= 4:
                            wv.append(ev_proj[-4])
                        tmm = None
                        for k_ in range(8):
                            tmm = sch.op("tensor", lambda e, t2=tt, q2=qg, k2=k_, b2=bnk: e.matmul(
                                pa(b2), xnf[:, k2, 128 * t2:128 * (t2 + 1)],
                                wv_t[:, k2, 512 * q2:512 * (q2 + 1)],
                                start=(k2 == 0), stop=(k2 == 7),
                                skip_group_check=True), waits=wv, inc="pe")
                        tev = sch.op("scalar", lambda e, t2=tt, q2=qg, b2=bnk: e.copy(
                            out=vt[:, t2, 512 * q2:512 * (q2 + 1)], in_=pa(b2)),
                            waits=[tmm], inc="se")
                        ev_proj.append(tev)
                        tv.append(tev)
                        tile_idx += 1
                t_pa_free = ev_proj[-1]
                tap("qf", qf[:], [tq[-1]])
                tap("vt", vt, [tv[-1]])

                if stage_a < 6:
                    continue
                def sigma(src, dst, wv):
                    a = sch.op("vector", lambda e: e.tensor_scalar(
                        out=tmp1, in0=src[:], scalar1=0.0, scalar2=None,
                        op0=Alu.min), waits=wv, inc="ve")
                    b = sch.op("scalar", lambda e: e.activation(
                        out=tmp1, in_=tmp1, func=Act.Exp),
                        waits=[a], inc="se")
                    return sch.op("vector", lambda e: e.scalar_tensor_tensor(
                        out=dst[:], in0=src[:], scalar=0.0, in1=tmp1,
                        op0=Alu.max, op1=Alu.add), waits=[b], inc="ve")

                t_sq_ = sigma(qf, sqf, [tq[-1], t_q1dma])
                t_sk_ = sigma(kf, skf, [tk[-1], t_sq_])
                tap("sqf", sqf[:], [t_sq_])
                tap("skf", skf[:], [t_sk_])

                t_swv = [sch.dma("scalar", vt_sw[0:64], vt[64:128],
                                 waits=[tv[-1], ev_q1[-1], t_segloop_prev,
                                        t_r2]),
                         sch.dma("scalar", vt_sw[64:128], vt[0:64],
                                 waits=[tv[-1], ev_q1[-1], t_segloop_prev,
                                        t_r2])]
                t_swaps = list(t_swv)

                if stage_a < 7:
                    continue
                ev_skt = []
                for g in range(8):
                    tt_, ka = divmod(4 * g, 8)
                    bnk = g % 2
                    wv = [t_sk_, t_pa_free]
                    if g >= 2:
                        wv.append(ev_skt[g - 2])
                    tmm = None
                    for j in range(4):
                        k_ = ka + j
                        tmm = sch.op("tensor", lambda e, t2=tt_, k2=k_, b2=bnk, j2=j: e.transpose(
                            pab(b2)[:, 128 * j2:128 * (j2 + 1)],
                            skf[:, k2, 128 * t2:128 * (t2 + 1)], idb[:]),
                            waits=wv, inc="pe")
                    ev_skt.append(sch.op("scalar", lambda e, t2=tt_, k2=ka, b2=bnk: e.copy(
                        out=skt[:, t2, 512 * (k2 // 4):512 * (k2 // 4 + 1)],
                        in_=pab(b2)[:, 0:512]), waits=[tmm], inc="se"))
                t_skt = ev_skt[-1]
                t_pa_free = t_skt
                tap("skt", skt, [t_skt])
                t_swskt = sch.dma("scalar", skt_sw[0:64], skt[64:128],
                                  waits=[t_skt, t_sk_, t_q1dma])
                t_swaps.append(t_swskt)

                t_zc = sch.op("vector", lambda e: e.tensor_reduce(
                    out=zc[:].rearrange("p (k l) -> p k l", l=8),
                    in_=skf[:].rearrange("p k (l j) -> p k l j", j=64),
                    axis=AX, op=Alu.add), waits=[t_sk_], inc="ve")

                if stage_a < 8:
                    continue
                t_zmc = None
                for ls in range(8):
                    for hp in range(2):
                        t_zmc = sch.op("vector", lambda e, h2=64 * hp, c0=hp, l=ls: e.tensor_copy(
                            out=zmm[h2:h2 + 64, l, c0:c0 + 15:2],
                            in_=zsb[h2:h2 + 64, :]),
                            waits=[t_zc, t_z0, t_den_prev, t_zmm0], inc="ve")
                    t_z0 = sch.op("gpsimd", lambda e, l=ls: e.tensor_tensor(
                        out=zsb[:], in0=zsb[:], in1=zc[:, l:l + 57:8],
                        op=Alu.add), waits=[t_zmc], inc="po")
                t_den = None
                first = True
                for ls in range(8):
                    for side, ssrc in ((0, sqf), (1, skf)):
                        for r in range(8):
                            t_den = sch.op("tensor", lambda e, s3=ssrc, r2=r, o=32 * ls + 16 * side + 2 * r, f=first, l=ls: e.matmul(
                                pac(0, o, o + 2), s3[:, r2, 64 * l:64 * l + 64],
                                zmm[:, l, 2 * r2:2 * r2 + 2], start=f,
                                stop=(o == 254), skip_group_check=True),
                                waits=[t_zmc, t_sq_, t_sk_, t_den_prev,
                                       t_pa_free, t_skt], inc="pe")
                            first = False
                t_dev = sch.op("scalar", lambda e: e.copy(
                    out=denb[0:64].rearrange("p a b -> p (a b)"),
                    in_=pac(0, 0, 256)), waits=[t_den], inc="se")
                t_rd = sch.op("vector", lambda e: e.tensor_scalar(
                    out=denb[0:64].rearrange("p a b -> p (a b)"),
                    in0=denb[0:64].rearrange("p a b -> p (a b)"),
                    scalar1=EPS_ATTN, scalar2=None, op0=Alu.add),
                    waits=[t_dev], inc="ve")
                t_rd = sch.op("vector", lambda e: e.reciprocal(
                    out=denb[0:64].rearrange("p a b -> p (a b)"),
                    in_=denb[0:64].rearrange("p a b -> p (a b)")),
                    waits=[t_rd], inc="ve")
                t_rdk = sch.op("vector", lambda e: e.tensor_scalar(
                    out=denb[0:64, :, 16:32], in0=denb[0:64, :, 16:32],
                    scalar1=-1.0, scalar2=None, op0=Alu.mult),
                    waits=[t_rd], inc="ve")
                t_nq = sch.op("vector", lambda e: e.tensor_tensor(
                    out=nqb[0:64], in0=denb[0:64, :, 0:16],
                    in1=bt[0:64, :].unsqueeze(1).to_broadcast([64, 8, 16]),
                    op=Alu.mult), waits=[t_rd, t_btm], inc="ve")
                t_den_prev = t_den

                t_sc_prev = t_pa_free
                t_exp = t_pa_free
                t_as = [t_pa_free]
                t_t2 = [t_pb_free]
                t_vm1 = t_pb_free
                t_ptr_prev = [t_dev, t_dev]
                t_exp_hist = [t_dev, t_dev]
                t_as_hist = [t_pa_free, t_pa_free]
                for ls in range(max_seg):
                    u2, sp = ls // 2, ls % 2
                    p2 = ls % 2
                    eff = seg_stage if ls < 2 else min(seg_stage, seg2_stage)
                    if eff < 2:
                        continue
                    t_kz = []
                    for hp in range(2):
                        t_kz.append(sch.op("gpsimd", lambda e, h2=64 * hp, hp2=hp, l=ls: e.tensor_copy(
                            out=kz[h2:h2 + 64, hp2:hp2 + 15:2, :],
                            in_=kf[h2:h2 + 64, :, 64 * l:64 * l + 64]),
                            waits=[t_sk_, t_kz0, t_sc_prev], inc="po"))
                    t_sc = None
                    for h in range(16):
                        t_sc = sch.op("tensor", lambda e, h2=h, l=ls, b0=2 * p2: e.matmul(
                            pac(b0 + h2 // 8, 64 * (h2 % 8), 64 * (h2 % 8) + 64),
                            qf[:, h2 // 2, 64 * l:64 * l + 64],
                            kz[:, h2, :],
                            start=(h2 % 8 == 0), stop=(h2 % 8 == 7),
                            skip_group_check=True),
                            waits=[tq[-1], t_dev, t_as_hist[p2],
                                   t_ptr_prev[0], t_ptr_prev[1]] + t_kz,
                            inc="pe")
                    t_sc_prev = t_sc

                    if eff < 3:
                        continue
                    vA = pA[0:64, 1024 * p2:1024 * p2 + 1024].rearrange(
                        "p (s j) -> p s j", j=64)
                    vP = psb[0:64, :].rearrange("p (s j) -> p s j", j=64)
                    t_mx = sch.op("vector", lambda e, v=vA: e.tensor_reduce(
                        out=mxs[0:64, :], in_=v, axis=AX, op=Alu.max),
                        waits=[t_sc], inc="ve")
                    t_sb2 = sch.op("vector", lambda e, v=vA: e.tensor_tensor(
                        out=v, in0=v,
                        in1=mxs[0:64, :].unsqueeze(2).to_broadcast([64, 16, 64]),
                        op=Alu.subtract), waits=[t_mx], inc="ve")
                    t_exp = sch.op("scalar", lambda e, v=vA, w=vP: e.activation(
                        out=w, in_=v, func=Act.Exp, scale=0.125),
                        waits=[t_sb2, t_ptr_prev[0], t_ptr_prev[1]], inc="se")
                    t_exp_hist[p2] = t_exp
                    t_mk = sch.op("vector", lambda e, w=vP: e.tensor_tensor(
                        out=w, in0=w,
                        in1=m01[0:64, :].unsqueeze(1).to_broadcast([64, 16, 64]),
                        op=Alu.mult), waits=[t_exp], inc="ve")
                    t_sm = sch.op("vector", lambda e, w=vP: e.tensor_reduce(
                        out=sms[0:64, :], in_=w, axis=AX, op=Alu.add),
                        waits=[t_mk], inc="ve")
                    t_rs = sch.op("vector", lambda e: e.reciprocal(
                        out=sms[0:64, :], in_=sms[0:64, :]), waits=[t_sm],
                        inc="ve")
                    t_ns = sch.op("vector", lambda e: e.tensor_tensor(
                        out=nsm[0:64, :], in0=sms[0:64, :], in1=btm[0:64, :],
                        op=Alu.mult), waits=[t_rs], inc="ve")
                    t_pn = sch.op("vector", lambda e, w=vP: e.tensor_tensor(
                        out=w, in0=w,
                        in1=nsm[0:64, :].unsqueeze(2).to_broadcast([64, 16, 64]),
                        op=Alu.mult), waits=[t_ns], inc="ve")

                    if eff < 4:
                        continue
                    t_pt = None
                    for h in range(16):
                        t_pt = sch.op("tensor", lambda e, h2=h, b2=2 * (1 - p2): e.transpose(
                            pab(b2)[0:64, 64 * h2:64 * h2 + 64],
                            psb[0:64, 64 * h2:64 * h2 + 64],
                            idb[0:64, 0:64]),
                            waits=[t_pn, t_as_hist[1 - p2], t_exp_hist[1 - p2],
                                   t_ptr_prev[1 - p2]], inc="pe")
                    t_ptr = sch.op("scalar", lambda e, b2=2 * (1 - p2): e.copy(
                        out=ptr[0:64, :], in_=pab(b2)[0:64, 0:1024]),
                        waits=[t_pt, t_ptr0], inc="se")
                    t_ptr_prev[1 - p2] = t_ptr

                    if eff < 5:
                        continue
                    t_ad = None
                    vsel = vt if sp == 0 else vt_sw
                    for h in range(16):
                        t_ad = sch.op("tensor", lambda e, h2=h, v4=vsel, u3=u2, b0=2 * p2: e.matmul(
                            pac(b0 + h2 // 8, 64 * (h2 % 8), 64 * (h2 % 8) + 64),
                            ptr[:, 64 * h2:64 * h2 + 64],
                            v4[:, u3, 64 * h2:64 * h2 + 64],
                            start=(h2 % 8 == 0), stop=(h2 % 8 == 7),
                            skip_group_check=True),
                            waits=[t_ptr, t_exp, t_swv[0], t_swv[1]], inc="pe")

                    if os.environ.get("PBDUMP") and ls == 1:
                        t_ad = sch.op("scalar", lambda e: e.copy(
                            out=outsb[0:64, 1, :], in_=pA[0:64, 1024:2048]),
                            waits=[t_ad], inc="se")
                    if eff < 6:
                        continue
                    t_amem = None
                    for h in range(16):
                        t_amem = sch.op("tensor", lambda e, h2=h, l=ls: e.matmul(
                            pbc(h2 // 8, 64 * (h2 % 8), 64 * (h2 % 8) + 64),
                            sqf[:, h2 // 2, 64 * l:64 * l + 64],
                            msbf[:, h2, :],
                            start=(h2 % 8 == 0), stop=(h2 % 8 == 7),
                            skip_group_check=True),
                            waits=[t_mcast, t_t2[-1]], inc="pe")
                    t_vp = None
                    for h in range(16):
                        t_vp = sch.op("tensor", lambda e, h2=h, l=ls: e.matmul(
                            pbc(2 + h2 // 8, 64 * (h2 % 8), 64 * (h2 % 8) + 64),
                            skf[:, h2 // 2, 64 * l:64 * l + 64],
                            msbf[:, h2, :],
                            start=(h2 % 8 == 0), stop=(h2 % 8 == 7),
                            skip_group_check=True),
                            waits=[t_mcast, t_vm1], inc="pe")

                    if eff < 7:
                        continue
                    if os.environ.get("PBDUMP") and ls == 1:
                        t_amem = sch.op("scalar", lambda e: e.copy(
                            out=outsb[0:64, 0, :], in_=pB[0:64, 0:1024]),
                            waits=[t_amem], inc="se")
                    vc = vt if sp == 0 else vt_sw
                    t_vm1 = sch.op("vector", lambda e, l=ls: e.tensor_tensor(
                        out=vmvp[0:64, :].rearrange("p (s j) -> p s j", j=64),
                        in0=pB[0:64, 1024:2048].rearrange("p (s j) -> p s j", j=64),
                        in1=denb[0:64, l, 16:32].unsqueeze(2).to_broadcast([64, 16, 64]),
                        op=Alu.mult), waits=[t_vp, t_rdk], inc="ve")
                    t_vm2 = sch.op("vector", lambda e, v4=vc, u3=u2: e.tensor_tensor(
                        out=vmvp[0:64, :], in0=vmvp[0:64, :],
                        in1=v4[0:64, u3, :], op=Alu.add),
                        waits=[t_vm1], inc="ve")

                    t_as = [t_as[-1]]
                    t_as_hist[p2] = None
                    for qg in range(2):
                        t2a = sch.op("vector", lambda e, q2=qg, l=ls: e.tensor_tensor(
                            out=tas,
                            in0=pB[0:64, 512 * q2:512 * (q2 + 1)].rearrange("p (s j) -> p s j", j=64),
                            in1=nqb[0:64, l, 8 * q2:8 * q2 + 8].unsqueeze(2).to_broadcast([64, 8, 64]),
                            op=Alu.mult), waits=[t_amem, t_nq, t_as[-1]],
                            inc="ve")
                        t_t2.append(t2a)
                        t_as.append(sch.op("vector", lambda e, q2=qg, l=ls, b0=1024 * p2: e.tensor_tensor(
                            out=ast[0:64, l, 512 * q2:512 * (q2 + 1)],
                            in0=tas, in1=pA[0:64, b0 + 512 * q2:b0 + 512 * (q2 + 1)],
                            op=Alu.add), waits=[t2a, t_ad], inc="ve"))

                    if eff < 8:
                        continue
                    skc = skt if sp == 0 else skt_sw
                    t_dm = None
                    for h in range(16):
                        t_dm = sch.op("tensor", lambda e, h2=h, s4=skc, u3=u2: e.matmul(
                            pB[:, 64 * h2:64 * h2 + 64],
                            s4[:, u3, 128 * (h2 // 2):128 * (h2 // 2) + 128],
                            vmvp[:, 64 * h2:64 * h2 + 64],
                            start=(h2 % 8 == 0), stop=(h2 % 8 == 7),
                            skip_group_check=True),
                            waits=[t_vm2, t_swskt, t_madd] + t_t2[-2:],
                            inc="pe")
                    t_ma0 = sch.op("vector", lambda e: e.tensor_tensor(
                        out=m32[0:64, 0:15:2, :], in0=m32[0:64, 0:15:2, :],
                        in1=pB[0:64, 0:1024].rearrange(
                            "p (s j) -> p s j", j=64)[:, 0:15:2, :],
                        op=Alu.add), waits=[t_dm], inc="ve")
                    t_madd = sch.op("vector", lambda e: e.tensor_tensor(
                        out=m32[64:128, 1:16:2, :], in0=m32[64:128, 1:16:2, :],
                        in1=pB[64:128, 0:1024].rearrange(
                            "p (s j) -> p s j", j=64)[:, 1:16:2, :],
                        op=Alu.add), waits=[t_dm, t_ma0], inc="ve")
                    t_mcast = sch.op("scalar", lambda e: e.copy(
                        out=msbf[:], in_=m32[:]), waits=[t_madd, t_ma0],
                        inc="se")
                    t_as_hist[p2] = t_as[-1]
                t_pa_free = t_as[-1]
                t_segloop_prev = t_madd

                tap("ast", ast[:], [t_as[-1]])
                tap("m32", m32[:], [t_madd])

                if stage_a < 9:
                    continue
                ev_as = []
                for k_ in range(8):
                    wv = [t_as[-1], t_pa_free]
                    if k_ >= 2:
                        wv.append(ev_as[k_ - 2])
                    bnk = k_ % 2
                    tmm = None
                    for l in range(8):
                        tmm = sch.op("tensor", lambda e, k2=k_, l2=l, b2=bnk: e.transpose(
                            pab(b2)[:, 64 * l2:64 * l2 + 64],
                            ast[0:64, l2, 128 * k2:128 * (k2 + 1)],
                            idb[0:64, 0:64]), waits=wv, inc="pe")
                    ev_as.append(sch.op("scalar", lambda e, k2=k_, b2=bnk: e.copy(
                        out=asf[:, k2, :], in_=pab(b2)[:, 0:512]),
                        waits=[tmm], inc="se"))
                t_asf = ev_as[-1]
                tap("asf", asf, [t_asf])

                t_resid = []
                t_odma = {}
                oidx = 0
                for tt in range(4):
                    for qg in range(2):
                        bnk = oidx % 4
                        wv = [t_asf]
                        if len(t_resid) >= 4:
                            wv.append(t_resid[-4])
                        tmm = None
                        for k_ in range(8):
                            tmm = sch.op("tensor", lambda e, t2=tt, q2=qg, k2=k_, b2=bnk: e.matmul(
                                pa(b2), asf[:, k2, 128 * t2:128 * (t2 + 1)],
                                wo_t[:, k2, 512 * q2:512 * (q2 + 1)],
                                start=(k2 == 0), stop=(k2 == 7),
                                skip_group_check=True), waits=wv, inc="pe")
                        t_resid.append(sch.op("vector", lambda e, t2=tt, q2=qg, b2=bnk, xb2=xb: e.tensor_add(
                            out=outsb[:, t2 % 2, 512 * q2:512 * (q2 + 1)],
                            in0=pa(b2), in1=xb2[:, t2, 512 * q2:512 * (q2 + 1)]),
                            waits=[tmm, tx, t_odma.get(tt - 2, t_outdma_prev)],
                            inc="ve"))
                        oidx += 1
                        if qg == 1:
                            t_odma[tt] = sch.dma(
                                "sync", out3[u][:, tt, :],
                                outsb[:, tt % 2, :], waits=[t_resid[-1]])
                tok_xfree[u] = t_resid[-1]
                t_pa_free = t_resid[-1]
                t_outdma_prev = t_odma[3]
                t_out_toks.append(t_outdma_prev)

            bar1 = sch.barrier()

            blkA = eA.enter_context(nc.Block())
            sch.replay(blkA)

        with ExitStack() as eB:
            sbB = lambda n, sh, dt: eB.enter_context(nc.sbuf_tensor(n, sh, dt))
            w1t = sbB("w1t", [128, 8, DH], BF16)
            w2t = sbB("w2t", [128, 32, D], BF16)
            hbuf = sbB("hbuf", [128, 32, 258], F32)
            cvt = sbB("cvt", [128, 32, 128], F32)
            xq2 = sbB("xq2", [128, 32, 128], BF16)
            xq1w = [sbB(f"xq1w{i}", [128, 8, 258], BF16) for i in range(2)]
            grow = [sbB(f"grow{i}", [1, 258], F32) for i in range(2)]
            gsb = sbB("gsb", [128, 258], F32)
            cwsb = sbB("cwsb", [128, 96], F32)
            cbsb = sbB("cbsb", [128, 32], F32)
            rmax = sbB("rmaxb", [128, 128], F32)
            s2s = sbB("s2s", [128, 8], F32)
            s2c = sbB("s2c", [128, 128], F32)
            s2r = sbB("s2r", [1, 128], F32)
            s2b = sbB("s2b", [128, 128], F32)
            outf_s = sbB("outf", [128, D], F32)
            mgc = sbB("mgc", [128, 1], F32)

            t_w1t = sch.dma("sync", w1t[:],
                            w1ts_h.rearrange("(k p) c -> p k c", p=128),
                            waits=[bar1[0]])
            t_w2t = sch.dma("sync", w2t[:],
                            w2ts_h.rearrange("(k p) c -> p k c", p=128),
                            waits=[bar1[0]])
            t_cw = sch.dma("sync", cwsb[:], cwr_h, waits=[bar1[0]])
            t_cb = sch.dma("sync", cbsb[:], cbr_h, waits=[bar1[0]])
            t_mgc = sch.op("gpsimd", lambda e: e.memset(mgc[:], MAGIC),
                           waits=[bar1[0]], inc="po")

            def win_load(w, wv):
                c0 = WIN * w - 1
                lo, hi = max(c0, 0), min(c0 + 258, T)
                j0 = lo - c0
                buf, gbf = xq1w[w % 2], grow[w % 2]
                tks = [sch.dma("scalar", buf[:, :, j0:j0 + hi - lo],
                               xq1s3[:, :, lo:hi], waits=wv),
                       sch.dma("scalar", gbf[:, j0:j0 + hi - lo],
                               gs_h[:, lo:hi], waits=wv)]
                if j0 > 0:
                    tks.append(sch.op("gpsimd", lambda e, b=buf: e.memset(
                        b[:, :, 0:j0], 0.0), waits=wv, inc="po"))
                    tks.append(sch.op("gpsimd", lambda e, g=gbf: e.memset(
                        g[:, 0:j0], 0.0), waits=wv, inc="po"))
                if hi - c0 < 258:
                    tks.append(sch.op("gpsimd", lambda e, b=buf, j1=hi - c0: e.memset(
                        b[:, :, j1:258], 0.0), waits=wv, inc="po"))
                    tks.append(sch.op("gpsimd", lambda e, g=gbf, j1=hi - c0: e.memset(
                        g[:, j1:258], 0.0), waits=wv, inc="po"))
                return tks

            ld = [None] * (NWIN + 1)
            ld[0] = win_load(0, [bar1[0]])
            t_free_xq1w = [bar1[0]] * 2
            t_pb45_free = bar1[0]
            t_pb5_free = bar1[0]
            t_w2o_free = [bar1[0], bar1[0]]
            t_outf_free = bar1[0]
            t_cvt_free = bar1[0]
            t_hb_free = bar1[0]
            t_w2mm_prev = bar1[0]
            out2 = out_h.rearrange("(t p) d -> t p d", p=128)

            for w in range(max_win if do_b else 0):
                buf, gbf = xq1w[w % 2], grow[w % 2]
                if w + 1 < (max_win if do_b else 0):
                    ld[w + 1] = win_load(w + 1, [t_free_xq1w[(w + 1) % 2]])
                lw = ld[w]

                t_gb = sch.op("tensor", lambda e, g=gbf: e.matmul(
                    pb(0, 0, 258), r1row[:], g[:], start=True, stop=True),
                    waits=lw + [t_pb45_free], inc="pe")
                t_gsb = sch.op("scalar", lambda e: e.copy(
                    out=gsb[:], in_=pb(0, 0, 258)), waits=[t_gb, t_hb_free],
                    inc="se")

                ev_h = []
                for g in range(16):
                    bnk0 = 2 * (g % 2)
                    wv = [t_w1t] + lw
                    if g >= 2:
                        wv.append(ev_h[2 * g - 3])
                    tmm = [None, None]
                    for ci in range(2):
                        c = 2 * g + ci
                        for k_ in range(8):
                            tmm[ci] = sch.op("tensor", lambda e, c2=c, k2=k_, b2=bnk0 + ci, bu=buf: e.matmul(
                                pa(b2, 0, 258), w1t[:, k2, 128 * c2:128 * (c2 + 1)],
                                bu[:, k2, :], start=(k2 == 0), stop=(k2 == 7),
                                skip_group_check=True), waits=wv, inc="pe")
                    for ci in range(2):
                        c = 2 * g + ci
                        ev_h.append(sch.op("vector", lambda e, c2=c, b2=bnk0 + ci: e.tensor_tensor(
                            out=hbuf[:, c2, :], in0=pa(b2, 0, 258), in1=gsb[:],
                            op=Alu.mult), waits=[tmm[ci], t_gsb, t_hb_free],
                            inc="ve"))
                t_h = ev_h[-1]
                tap(f"hbuf_{w}", hbuf[:], [t_h])

                for a in range(2):
                    cc = 1 + 128 * a
                    t_ct = []
                    for r in range(32):
                        t1 = sch.op("scalar", lambda e, r2=r, c2=cc: e.activation(
                            out=cvt[:, r2, :], in_=hbuf[:, r2, c2:c2 + 128],
                            func=Act.Identity,
                            scale=cwsb[:, 3 * r2 + 1:3 * r2 + 2],
                            bias=cbsb[:, r2:r2 + 1]),
                            waits=[t_h, t_cw, t_cb, t_cvt_free],
                            inc="se")
                        t2 = sch.op("vector", lambda e, r2=r, c2=cc: e.scalar_tensor_tensor(
                            out=cvt[:, r2, :], in0=hbuf[:, r2, c2 - 1:c2 + 127],
                            scalar=cwsb[:, 3 * r2:3 * r2 + 1], in1=cvt[:, r2, :],
                            op0=Alu.mult, op1=Alu.add), waits=[t1], inc="ve")
                        t_ct.append(sch.op("vector", lambda e, r2=r, c2=cc: e.scalar_tensor_tensor(
                            out=cvt[:, r2, :], in0=hbuf[:, r2, c2 + 1:c2 + 129],
                            scalar=cwsb[:, 3 * r2 + 2:3 * r2 + 3],
                            in1=cvt[:, r2, :], op0=Alu.mult, op1=Alu.add),
                            waits=[t2], inc="ve"))
                    t_si = sch.op("scalar", lambda e: e.activation(
                        out=cvt[:], in_=cvt[:], func=Act.Silu),
                        waits=[t_ct[-1]], inc="se")
                    tap(f"cvt_{w}_{a}", cvt[:], [t_si])
                    t_rm = sch.op("vector", lambda e: e.tensor_reduce(
                        out=rmax[:], in_=cvt[:].transpose([0, 2, 1]), axis=AX,
                        op=Alu.max, apply_absolute_value=True),
                        waits=[t_si], inc="ve")
                    t_tr = sch.op("tensor", lambda e: e.transpose(
                        pb(1, 0, 128), rmax[:], idf[:]),
                        waits=[t_rm, t_pb5_free], inc="pe")
                    t_s2p = sch.op("vector", lambda e: e.tensor_reduce(
                        out=s2s[:, 0:1], in_=pb(1, 0, 128), axis=AX,
                        op=Alu.max), waits=[t_tr], inc="ve")
                    t_cl = sch.op("vector", lambda e: e.tensor_scalar(
                        out=s2s[:, 1:2], in0=s2s[:, 0:1], scalar1=1e-5,
                        scalar2=None, op0=Alu.max), waits=[t_s2p], inc="ve")
                    t_s2 = sch.op("vector", lambda e: e.reciprocal(
                        out=s2s[:, 2:3], in_=s2s[:, 1:2]), waits=[t_cl], inc="ve")
                    t_s2 = sch.op("vector", lambda e: e.tensor_scalar(
                        out=s2s[:, 2:3], in0=s2s[:, 2:3], scalar1=127.0,
                        scalar2=None, op0=Alu.mult), waits=[t_s2], inc="ve")
                    t_dq = sch.op("vector", lambda e: e.tensor_scalar(
                        out=s2s[:, 3:4], in0=s2s[:, 1:2], scalar1=1.0 / 127.0,
                        scalar2=None, op0=Alu.mult), waits=[t_cl], inc="ve")
                    t_dq = sch.op("vector", lambda e: e.tensor_tensor(
                        out=s2s[:, 4:5], in0=s2s[:, 3:4], in1=prs[:, 2:3],
                        op=Alu.mult), waits=[t_dq], inc="ve")
                    t_cp = sch.op("vector", lambda e: e.tensor_copy(
                        out=s2c[:, 0:1], in_=s2s[:, 2:3]), waits=[t_s2], inc="ve")
                    t_tr2 = sch.op("tensor", lambda e: e.transpose(
                        pb(1, 128, 256), s2c[:], idf[:]),
                        waits=[t_cp, t_s2p], inc="pe")
                    t_er = sch.op("scalar", lambda e: e.copy(
                        out=s2r[:], in_=pB[0:1, 512 * 1 + 128:512 * 1 + 256]),
                        waits=[t_tr2], inc="se")
                    t_pb5_free = t_er
                    t_bb = sch.op("tensor", lambda e: e.matmul(
                        pb(0, 258, 386), onesf[:], s2r[:], start=True,
                        stop=True, skip_group_check=True),
                        waits=[t_er, t_gb], inc="pe")
                    t_bs = sch.op("scalar", lambda e: e.copy(
                        out=s2b[:], in_=pb(0, 258, 386)), waits=[t_bb], inc="se")
                    t_m2 = sch.op("vector", lambda e: e.tensor_tensor(
                        out=cvt[:], in0=cvt[:],
                        in1=s2b[:].unsqueeze(1).to_broadcast([128, 32, 128]),
                        op=Alu.mult), waits=[t_bs, t_rm], inc="ve")
                    t_q2a = sch.op("scalar", lambda e: e.activation(
                        out=cvt[:], in_=cvt[:], func=Act.Identity,
                        bias=mgc[:, 0:1]), waits=[t_m2, t_mgc], inc="se")
                    t_q2 = sch.op("vector", lambda e: e.tensor_scalar(
                        out=xq2[:], in0=cvt[:], scalar1=MAGIC, scalar2=None,
                        op0=Alu.subtract),
                        waits=[t_q2a, t_w2mm_prev], inc="ve")
                    t_cvt_free = t_q2
                    tap(f"xq2_{w}_{a}", xq2[:], [t_q2])
                    t_ev2 = []
                    for qg in range(2):
                        tmm = None
                        for r in range(32):
                            tmm = sch.op("tensor", lambda e, r2=r, q2=qg: e.matmul(
                                pb(2 + q2), xq2[:, r2, :],
                                w2t[:, r2, 512 * q2:512 * (q2 + 1)],
                                start=(r2 == 0), stop=(r2 == 31),
                                skip_group_check=True),
                                waits=[t_q2, t_w2t, t_w2o_free[qg]], inc="pe")
                        t_ev2.append(sch.op("scalar", lambda e, q2=qg: e.activation(
                            out=outf_s[:, 512 * q2:512 * (q2 + 1)], in_=pb(2 + q2),
                            func=Act.Identity, scale=s2s[:, 4:5], bias=0.0),
                            waits=[tmm, t_dq, t_outf_free], inc="se"))
                        t_w2o_free[qg] = t_ev2[-1]
                    t_w2mm_prev = t_ev2[-1]
                    t_outf_free = sch.dma(
                        "gpsimd", out2[2 * w + a], outf_s[:], waits=t_ev2,
                        accum_op=Alu.add)
                t_hb_free = t_ct[-1]
                t_free_xq1w[w % 2] = ev_h[-1]
                t_pb45_free = t_bs

            sch.op("gpsimd", lambda e: e.nop(), waits=[("dgp", sch.cnt["dgp"])])
            sch.op("sync", lambda e: e.nop(), waits=[("dsy", sch.cnt["dsy"])])
            sch.op("scalar", lambda e: e.nop(), waits=[("dsc", sch.cnt["dsc"])])
            sch.barrier()

            blkB = eB.enter_context(nc.Block())
            sch.replay(blkB)

    return nc


_NC_CACHE = {}


def _get_nc(key, debug_tap=None):
    if key not in _NC_CACHE:
        _NC_CACHE[key] = build_nc(debug_tap)
    return _NC_CACHE[key]


def make_in_maps(inputs):
    x = np.ascontiguousarray(np.asarray(inputs["L0_attn_x"], dtype=np.float32))
    wqT = np.ascontiguousarray(np.asarray(inputs["L0_attn_W_q"], np.float32).T)
    wkT = np.ascontiguousarray(np.asarray(inputs["L0_attn_W_k"], np.float32).T)
    wvT = np.ascontiguousarray(np.asarray(inputs["L0_attn_W_v"], np.float32).T)
    woT = np.ascontiguousarray(np.asarray(inputs["L0_attn_W_o"], np.float32).T)
    beta = np.asarray(inputs["L0_attn_beta_param"], np.float32).reshape(1, H)
    w1T = np.ascontiguousarray(np.asarray(inputs["L0_ffn_w1"], np.float32).T)
    w2T = np.ascontiguousarray(np.asarray(inputs["L0_ffn_w2"], np.float32).T)
    cwr = np.ascontiguousarray(
        np.asarray(inputs["L0_ffn_conv_w"], np.float32).reshape(32, 128, 3)
        .transpose(1, 0, 2).reshape(128, 96))
    cbr = np.ascontiguousarray(
        np.asarray(inputs["L0_ffn_conv_b"], np.float32).reshape(32, 128).T)
    maps = []
    for b in range(8):
        maps.append({"x": np.ascontiguousarray(x[b]), "wqT": wqT, "wkT": wkT,
                     "wvT": wvT, "woT": woT, "beta": beta, "w1T": w1T,
                     "cwr": cwr, "cbr": cbr, "w2T": w2T})
    return maps


def kernel(**inputs):
    nc = _get_nc("main")
    maps = make_in_maps(inputs)
    res = run_bass_kernel_spmd(nc, maps, core_ids=list(range(8)))
    return np.stack([np.asarray(r["out"], dtype=np.float32)
                     for r in res.results], axis=0)
